# version 17
# baseline (speedup 1.0000x reference)
import math
from contextlib import ExitStack

import numpy as np

import concourse.bass as bass
import concourse.mybir as mybir
from concourse.bass_utils import run_bass_kernel_spmd

F32 = mybir.dt.float32
BF16 = mybir.dt.bfloat16
F8 = mybir.dt.float8e4
U8 = mybir.dt.uint8
U32 = mybir.dt.uint32
AF = mybir.ActivationFunctionType
ALU = mybir.AluOpType
AX = mybir.AxisListType

D = 2048
T = 2048
NCD = 16
NTC = 16
H, HK, HD = 16, 4, 128
NQ = 16
WINC = 640
I = 6144
NM = 48
EPS = 1e-5
QSCALE = 1.0 / math.sqrt(HD)
INV_D = 1.0 / D

DBG = []
_CACHE = {}


class Ctr:
    def __init__(self, stack, nc, name, step=1):
        self.h = stack.enter_context(nc.semaphore(name))
        self.step = step
        self.n = 0

    def bump(self):
        self.n += self.step
        return self.n


def build():
    nc = bass.Bass("TRN2", target_bir_lowering=False)
    stack = ExitStack()

    xin = nc.dram_tensor("x", [T, D], F32, kind="ExternalInput").ap()
    wq = nc.dram_tensor("wq", [D, H * HD], F32, kind="ExternalInput").ap()
    wk = nc.dram_tensor("wk", [D, HK * HD], F32, kind="ExternalInput").ap()
    wv = nc.dram_tensor("wv", [D, HK * HD], F32, kind="ExternalInput").ap()
    wo = nc.dram_tensor("wo", [H * HD, D], F32, kind="ExternalInput").ap()
    wmod = nc.dram_tensor("wmod", [D, 6 * D], F32, kind="ExternalInput").ap()
    gw = nc.dram_tensor("gw", [D, I], F32, kind="ExternalInput").ap()
    uw = nc.dram_tensor("uw", [D, I], F32, kind="ExternalInput").ap()
    dw = nc.dram_tensor("dw", [I, D], F32, kind="ExternalInput").ap()
    c_p = nc.dram_tensor("c_p", [128, 128], F32, kind="ExternalInput").ap()
    gs_p = nc.dram_tensor("gs_p", [128, 128], F32, kind="ExternalInput").ap()
    us_p = nc.dram_tensor("us_p", [128, 128], F32, kind="ExternalInput").ap()
    ds_p = nc.dram_tensor("ds_p", [128, 128], F32, kind="ExternalInput").ap()
    ident = nc.dram_tensor("ident", [128, 128], F32, kind="ExternalInput").ap()
    masks = nc.dram_tensor("masks", [5, 128, WINC], F32, kind="ExternalInput").ap()
    yout = nc.dram_tensor("y", [T, D], F32, kind="ExternalOutput").ap()

    mod_scr = nc.dram_tensor("mod_scr", [1, 6 * D], F32, kind="Internal").ap()
    g_scr = nc.dram_tensor("g_scr", [NM, 128, 2048], U8, kind="Internal").ap()
    u_scr = nc.dram_tensor("u_scr", [NM, 128, 2048], U8, kind="Internal").ap()
    d_scr = nc.dram_tensor("d_scr", [NCD, 128, 6144], U8, kind="Internal").ap()
    wo16 = nc.dram_tensor("wo16", [H * HD, D], BF16, kind="Internal").ap()

    BASE = 16640
    LIMIT = 229200
    cur = [BASE]

    OFFS = {}

    def sb(name, shape, dtype, off=None, align=64):
        bpp = int(np.prod(shape[1:])) * mybir.dt.size(dtype)
        if off is None:
            cur[0] = (cur[0] + align - 1) // align * align
            off = cur[0]
            cur[0] += bpp
        assert off + bpp <= LIMIT, (name, off, bpp)
        OFFS[name] = off
        return nc.alloc_sbuf_tensor_at(name, shape, dtype, offset=off)

    id_bf = sb("id_bf", [128, 128], BF16)
    id32 = sb("id32", [128, 128], F32)
    modall = sb("modall", [128, 96], F32)
    gsus = sb("gsus", [128, 96], F32)
    dsq = sb("dsq", [128, 16], F32)
    gbds = sb("gbds", [128, 16], F32)
    csi = sb("csi", [128, 16], BF16)
    tmpa = sb("tmpa", [128, 16], F32)
    tmpb = sb("tmpb", [128, 16], F32)
    tmpc = sb("tmpc", [128, 16], F32)
    sumx = sb("sumx", [128, 16], F32)
    ssq = sb("ssq", [128, 16], F32)
    smean = sb("smean", [128, 16], F32)
    svar = sb("svar", [128, 16], F32)
    ssd = sb("ssd", [128, 16], F32)
    srstd = sb("srstd", [128, 16], F32)
    negmax = sb("negmax", [128, 2], F32)
    esum = sb("esum", [128, 2], F32)
    rsum = sb("rsum", [128, 2], F32)
    vecs = sb("vecs", [128, 128], F32)
    x3p = sb("x3p", [128, 4, 512], F32)
    sy2 = sb("sy2", [128, 2, 512], BF16)
    PERS_END = cur[0]

    O_ATN = (PERS_END + 63) // 64 * 64
    s_masks = sb("s_masks", [128, 5, WINC], BF16, off=O_ATN)
    s_p = sb("s_p", [128, 2, WINC], BF16, off=O_ATN + 6400)
    s_pt = sb("s_pt", [128, 2, WINC], BF16, off=O_ATN + 6400 + 2560)
    s_d = sb("s_d", [128, 2, 128], BF16, off=O_ATN + 6400 + 5120)
    d8 = sb("d8", [128, 2, 6144], U8, off=O_ATN)
    O_WBUF = O_ATN + 12544

    wblk = sb("wblk", [128, 2, 16, 128], BF16, off=O_WBUF)
    g8 = sb("g8", [128, 2, 2048], U8, off=O_WBUF)
    u8w = sb("u8w", [128, 2, 2048], U8, off=O_WBUF + 4096)
    O_KV = O_WBUF + 8192

    k_fm = sb("k_fm", [128, HK, T + 512], BF16, off=O_KV)
    v_tm = sb("v_tm", [128, NTC + 4, 512], BF16, off=O_KV + 20480)
    s_u2 = sb("s_u2", [128, 2, D], F32, off=O_KV)
    s_xh2 = sb("s_xh2", [128, 4, D], BF16, off=O_KV + 16384)
    sq2 = sb("sq2", [128, D], F32, off=O_KV + 32768)
    xhq2 = sb("xhq2", [128, 4, D], BF16, off=O_KV + 16384)
    xhq3 = sb("xhq3", [128, 4, D], BF16, off=O_KV)
    O_A = O_KV + 41088

    wmodb = sb("wmodb", [128, 4, 3072], BF16, off=O_A)
    h1 = sb("h1", [128, NCD, T], BF16, off=O_A)
    attn = sb("attn", [128, H, T], BF16, off=O_A)
    fuse = sb("fuse", [128, NM, 512], BF16, off=O_A)
    h2 = sb("h2", [128, NCD, 512], BF16, off=O_A + 49152)
    s_xe2 = sb("s_xe2", [128, 4, 2, 512], F32, off=O_A + 49152)
    O_B = O_A + 65536

    s_x1 = sb("s_x1", [128, 2, D], F32, off=O_B)
    xhq = sb("xhq", [128, 4, D], BF16, off=O_B + 16384)
    sq1 = sb("sq1", [128, D], F32, off=O_B + 32768)
    modtmp = sb("modtmp", [1, 3072], F32, off=O_B + 40960)
    wv_rs = sb("wv_rs", [128, 16, 512], BF16, off=O_B + 16384)
    q_fm = sb("q_fm", [128, H, T], BF16, off=O_B)
    contrib = sb("contrib", [128, NCD, T], BF16, off=O_B)
    O_TOP = O_B + 65536
    pre8 = sb("pre8", [128, 3, 2048], F8, off=O_TOP)
    assert O_TOP + 6144 <= LIMIT

    PSf = nc.alloc_psum_tensor("PSf", [128, 4096], F32)

    def bankf(i, w=512):
        return PSf[:, 512 * i:512 * i + w]

    OPS = {k: [] for k in ("pe", "ve", "se", "gp", "sy")}

    def C(name, step=1):
        return Ctr(stack, nc, name, step)

    dCn = C("dCn", 16)
    dWm = [C("dWm0", 16), C("dWm1", 16), C("dWm2", 16), C("dWm3", 16)]
    dWb = [C("dWb0", 16), C("dWb1", 16)]
    dPre = C("dPre", 16)
    dScr = C("dScr", 16)
    dMT = C("dMT", 16)
    dX = [C("dX0", 16), C("dX1", 16)]
    dXE = C("dXE", 16)
    dG = [C("dG0", 16), C("dG1", 16)]
    dU = [C("dU0", 16), C("dU1", 16)]
    dD8 = [C("dD80", 16), C("dD81", 16)]
    dWo = [C("dWo0", 16), C("dWo1", 16)]
    dPre2 = [C("dPreA", 16), C("dPreB", 16)]
    dScr2 = [C("dScrA", 16), C("dScrB", 16)]
    dO = C("dO", 16)
    dOL = [C("dO0", 16), C("dO1", 16), C("dO2", 16), C("dO3", 16)]
    dWv = C("dWv", 16)

    c_pe = C("c_pe")
    c_ve = C("c_ve")
    c_se = C("c_se")
    c_gp = C("c_gp")
    c_mod = C("c_mod")
    c_sgn = C("c_sgn")
    c_sq = C("c_sq")
    c_ssd = C("c_ssd")
    c_vst = C("c_vst")
    c_vr = C("c_vr")
    c_vxh = C("c_vxh")
    c_tr = C("c_tr")
    c_ev = C("c_ev")
    c_lau = C("c_lau")
    c_qk = C("c_qk")
    c_max = C("c_max")
    c_exp = C("c_exp")
    c_vd = C("c_vd")
    c_pt = C("c_pt")
    c_pte = C("c_pte")
    c_av = C("c_av")
    c_ave = C("c_ave")
    c_wo = C("c_wo")
    c_fpe = C("c_fpe")
    c_fse = C("c_fse")
    c_fve = C("c_fve")
    c_h2 = C("c_h2")
    c_ut = C("c_ut")

    def ev(eng, fn, inc=None, waits=(), sc=None):
        waits = list(waits)
        if sc is not None:
            waits.append((sc, sc.n))
            if inc is None:
                inc = sc
        w = tuple((c.h, v) for (c, v) in waits if v > 0)
        tgt = None
        if inc is not None:
            inc.bump()
            tgt = (inc.h, inc.step)

        def run(e, fn=fn, w=w, tgt=tgt):
            for hh, vv in w:
                e.wait_ge(hh, vv)
            r = fn(e)
            if tgt is not None:
                r.then_inc(tgt[0], tgt[1])
        OPS[eng].append(run)

    bank_rel = {i: None for i in range(8)}

    def bw(i):
        r = bank_rel[i]
        return [] if r is None else [(r[0], r[1])]

    def brel(i, ctr):
        bank_rel[i] = (ctr, ctr.n)

    def _assemble():
        with nc.Block() as block:
            @block.sync
            def _(e):
                for f in OPS["sy"]:
                    f(e)

            @block.gpsimd
            def _(e):
                for f in OPS["gp"]:
                    f(e)

            @block.tensor
            def _(e):
                for f in OPS["pe"]:
                    f(e)

            @block.vector
            def _(e):
                for f in OPS["ve"]:
                    f(e)

            @block.scalar
            def _(e):
                for f in OPS["se"]:
                    f(e)

    ev("gp", lambda e: e.dma_start(id_bf[:, :], ident), sc=dCn)
    ev("gp", lambda e: e.dma_start(id32[:, :], ident), sc=dCn)
    ID_OK = dCn.n
    for vi, src in enumerate((c_p, gs_p, us_p, ds_p)):
        ev("gp", lambda e, s=src: e.dma_start(vecs[:, :], s),
           sc=dCn, waits=[(c_pe, vi)])
        ev("pe", lambda e: e.matmul(bankf(5, 128), vecs[:, :], id32[:, :],
                                    start=True, stop=True),
           inc=c_pe, waits=[(dCn, dCn.n)] + bw(5))
        if vi == 0:
            ev("se", lambda e: e.activation(csi[:, :], bankf(5, 16), AF.Silu,
                                            bias=0.0, scale=1.0),
               inc=c_ev, waits=[(c_pe, c_pe.n)])
            brel(5, c_ev)
        elif vi == 1:
            ev("se", lambda e: e.copy(gsus[:, 0:48], bankf(5, 48)),
               inc=c_ev, waits=[(c_pe, c_pe.n)])
            brel(5, c_ev)
        elif vi == 2:
            ev("se", lambda e: e.copy(gsus[:, 48:96], bankf(5, 48)),
               inc=c_ev, waits=[(c_pe, c_pe.n)])
            brel(5, c_ev)
        else:
            ev("se", lambda e: e.copy(dsq[:, :], bankf(5, 16)),
               inc=c_ev, waits=[(c_pe, c_pe.n)])
            brel(5, c_ev)
    VEC_EV = c_ev.n

    XHB = [xhq, xhq2, xhq3, xhq2]
    ln_tr_done = [0] * 9
    xld = {}
    xh_done = {}

    def emit_xload(tc):
        par = tc % 2
        rel = [(c_sq, tc - 1), (c_vxh, tc - 1)]
        ev("sy", lambda e, tc=tc, par=par: e.dma_start(
            s_x1[:, par, :], xin[tc * 128:(tc + 1) * 128, :]),
           sc=dX[par], waits=rel)
        xld[tc] = (dX[par], dX[par].n)

    def emit_stats(tc):
        par = tc % 2
        nx = xld[tc]
        ev("ve", lambda e, tc=tc, par=par: e.reduce_sum(
            sumx[:, tc:tc + 1], s_x1[:, par, :], AX.X),
           waits=[nx], sc=c_ve)
        ev("se", lambda e, tc=tc, par=par: e.activation(
            sq1[:, :], s_x1[:, par, :], AF.Square, bias=0.0, scale=1.0,
            accum_out=ssq[:, tc:tc + 1]),
           sc=c_sq, waits=[nx])
        ev("ve", lambda e, tc=tc: e.tensor_scalar(
            smean[:, tc:tc + 1], sumx[:, tc:tc + 1], INV_D, None, ALU.mult),
           waits=[(c_sq, tc + 1)], sc=c_ve)
        ev("ve", lambda e, tc=tc: e.tensor_scalar(
            svar[:, tc:tc + 1], ssq[:, tc:tc + 1], INV_D, EPS, ALU.mult,
            ALU.add), sc=c_ve)
        ev("ve", lambda e, tc=tc: e.tensor_tensor(
            tmpa[:, tc:tc + 1], smean[:, tc:tc + 1], smean[:, tc:tc + 1],
            ALU.mult), sc=c_ve)
        ev("ve", lambda e, tc=tc: e.tensor_tensor(
            svar[:, tc:tc + 1], svar[:, tc:tc + 1], tmpa[:, tc:tc + 1],
            ALU.subtract), sc=c_ve, inc=c_vst)
        ev("se", lambda e, tc=tc: e.sqrt(ssd[:, tc:tc + 1], svar[:, tc:tc + 1]),
           inc=c_ssd, waits=[(c_vst, c_vst.n)])
        ev("ve", lambda e, tc=tc: e.reciprocal(srstd[:, tc:tc + 1],
                                               ssd[:, tc:tc + 1]),
           inc=c_vr, waits=[(c_ssd, c_ssd.n)])

    def emit_xhat(tc, extra=()):
        par = tc % 2
        q4 = tc % 4
        qq = tc // 4
        buf = XHB[qq]
        ev("ve", lambda e, tc=tc, par=par, q4=q4, buf=buf: e.tensor_scalar(
            buf[:, q4, :], s_x1[:, par, :], smean[:, tc:tc + 1],
            srstd[:, tc:tc + 1], ALU.subtract, ALU.mult),
           inc=c_vxh, waits=[(c_vr, c_vr.n)] + list(extra))
        if q4 == 3:
            xh_done[qq] = c_vxh.n

    for tc in range(14):
        emit_xload(tc)
        emit_stats(tc)
        if tc < 12:
            emit_xhat(tc)

    wk_ld = {}
    for kh in (0, 1):
        ev("gp", lambda e, kh=kh: e.dma_start(
            wblk[:, kh, :, :],
            wk[:, kh * 128:(kh + 1) * 128].rearrange("(c p) n -> p c n",
                                                     p=128)),
           sc=dWb[kh])
        wk_ld[kh] = dWb[kh].n

    chunk_done = {}
    for g in range(4):
        for kc in range(NCD):
            par = kc % 4
            idx = g * NCD + kc
            wfree = [(c_pe, chunk_done[idx - 4])] if idx >= 4 else []
            ev("gp", lambda e, g=g, kc=kc, par=par: e.dma_start(
                wmodb[:, par, :],
                wmod[kc * 128:(kc + 1) * 128, g * 3072:(g + 1) * 3072]),
               sc=dWm[par], waits=wfree)
            ld = dWm[par].n
            for nt in range(6):
                w0 = []
                if nt == 0:
                    w0 = [(dWm[par], ld), (c_ev, VEC_EV)]
                    if kc == 0:
                        w0 += bw(0) + bw(1) + bw(2) + bw(3) + bw(4) + bw(5)
                ev("pe", lambda e, kc=kc, par=par, nt=nt: e.matmul(
                    PSf[0:1, nt * 512:(nt + 1) * 512], csi[:, kc:kc + 1],
                    wmodb[:, par, nt * 512:(nt + 1) * 512],
                    start=(kc == 0), stop=(kc == NCD - 1)),
                   inc=(c_pe if nt == 5 else None), waits=w0)
            chunk_done[idx] = c_pe.n
        ev("ve", lambda e: e.tensor_copy(modtmp[:, :], PSf[0:1, 0:3072]),
           inc=c_ve, waits=[(c_pe, c_pe.n), (dMT, dMT.n)])
        ev("gp", lambda e, g=g: e.dma_start(
            mod_scr[:, g * 3072:(g + 1) * 3072], modtmp[:, :]),
           sc=dMT, waits=[(c_ve, c_ve.n)])
        for i in range(6):
            brel(i, c_ve)
    ev("gp", lambda e: e.dma_start(
        vecs[0:96, :], mod_scr.rearrange("a (r n) -> (a r) n", n=128)),
       sc=dMT, waits=[(c_pe, c_pe.n)])
    ev("pe", lambda e: e.matmul(bankf(5, 128), vecs[:, :], id32[:, :],
                                start=True, stop=True),
       inc=c_pe, waits=[(dMT, dMT.n)] + bw(5))
    ev("ve", lambda e: e.tensor_copy(modall[:, :], bankf(5, 96)),
       inc=c_ve, waits=[(c_pe, c_pe.n)])
    brel(5, c_ve)
    MODCP = c_ve.n
    ev("ve", lambda e: e.tensor_scalar(modall[:, 16:32], modall[:, 16:32],
                                       1.0, None, ALU.add), sc=c_ve)
    ev("ve", lambda e: e.tensor_scalar(modall[:, 64:80], modall[:, 64:80],
                                       1.0, None, ALU.add), sc=c_ve)
    for b_ in (32, 80):
        ev("se", lambda e, b=b_: e.activation(
            tmpb[:, :], modall[:, b:b + 16], AF.Sign, bias=0.0, scale=1.0),
           inc=c_sgn, waits=[(c_ve, c_ve.n)])
        ev("ve", lambda e: e.tensor_tensor(tmpc[:, :], tmpb[:, :], tmpb[:, :],
                                           ALU.mult), sc=c_ve,
           waits=[(c_sgn, c_sgn.n)])
        ev("ve", lambda e: e.scalar_tensor_tensor(
            tmpb[:, :], tmpb[:, :], 1.0, tmpc[:, :], ALU.add, ALU.subtract),
           sc=c_ve)
        ev("ve", lambda e, b=b_: e.tensor_tensor(
            tmpa[:, :], modall[:, b:b + 16], tmpb[:, :], ALU.mult), sc=c_ve)
        ev("ve", lambda e: e.tensor_scalar(
            tmpa[:, :], tmpa[:, :], 0.25, None, ALU.max), sc=c_ve)
        ev("ve", lambda e, b=b_: e.tensor_tensor(
            modall[:, b:b + 16], tmpa[:, :], tmpb[:, :], ALU.mult), sc=c_ve)
    ev("ve", lambda e: e.tensor_tensor(gbds[:, :], modall[:, 80:96], dsq[:, :],
                                       ALU.mult), sc=c_ve, inc=c_mod)

    ev("gp", lambda e: e.dma_start(s_masks[:, :, :],
                                   masks.rearrange("v p n -> p v n")),
       sc=dCn)
    MASKS_OK = dCn.n

    def emit_trq(qq):
        buf = XHB[qq]
        for dc in range(NCD):
            bk = dc % 2
            for t4 in range(4):
                w0 = []
                if t4 == 0:
                    w0 = [(c_vxh, xh_done[qq]), (dCn, ID_OK)] + bw(6 + bk)
                ev("pe", lambda e, t4=t4, dc=dc, bk=bk, buf=buf: e.matmul(
                    bankf(6 + bk)[:, t4 * 128:(t4 + 1) * 128],
                    buf[:, t4, dc * 128:(dc + 1) * 128], id_bf[:, :],
                    start=(t4 == 0), stop=(t4 == 3)),
                   inc=(c_tr if t4 == 3 else None), waits=w0)
            ev("se", lambda e, dc=dc, bk=bk, qq=qq: e.activation(
                h1[:, dc, qq * 512:(qq + 1) * 512], bankf(6 + bk),
                AF.Identity, bias=modall[:, dc:dc + 1],
                scale=modall[:, 16 + dc:17 + dc]),
               inc=c_ev, waits=[(c_tr, c_tr.n), (c_mod, 1)])
            brel(6 + bk, c_ev)
        ln_tr_done[qq + 1] = c_tr.n

    emit_trq(0)
    emit_trq(1)
    TRQ1 = ln_tr_done[2]
    emit_xhat(12, extra=[(c_tr, TRQ1)])
    emit_xhat(13, extra=[(c_tr, TRQ1)])
    emit_xload(14)
    emit_stats(14)
    emit_xhat(14, extra=[(c_tr, TRQ1)])
    emit_xload(15)
    emit_stats(15)
    emit_xhat(15, extra=[(c_tr, TRQ1)])
    emit_trq(2)
    emit_trq(3)
    H1_EV = c_ev.n
    LN1_TR = c_tr.n

    ev("gp", lambda e: e.memset(k_fm[:, :, 0:512], 0.0), inc=c_gp,
       waits=[(c_tr, LN1_TR)])
    ev("gp", lambda e: e.memset(v_tm[:, 0:4, :], 0.0), inc=c_gp)
    PADS = c_gp.n

    ev("gp", lambda e: e.dma_start(
        wv_rs[:, :, :], wv.rearrange("(c p) n -> p c n", p=128)),
       sc=dWv, waits=[(c_tr, ln_tr_done[1])])
    WV_OK = dWv.n
    ev("gp", lambda e: e.dma_start(wo16, wo), sc=dCn)
    WO16_OK = dCn.n

    pj = 0
    kh_done = {}
    for kh in range(HK):
        par = kh % 2
        if kh < 2:
            ld = wk_ld[kh]
        else:
            ev("gp", lambda e, kh=kh, par=par: e.dma_start(
                wblk[:, par, :, :],
                wk[:, kh * 128:(kh + 1) * 128].rearrange("(c p) n -> p c n",
                                                         p=128)),
               sc=dWb[par], waits=[(c_pe, kh_done[kh - 2])])
            ld = dWb[par].n
        for tt in range(4):
            bk = 2 + pj % 4
            for kc in range(NCD):
                w0 = []
                if kc == 0:
                    w0 = [(dWb[par], ld), (c_ev, H1_EV)] + bw(bk)
                ev("pe", lambda e, par=par, tt=tt, kc=kc, bk=bk: e.matmul(
                    bankf(bk), wblk[:, par, kc, :],
                    h1[:, kc, tt * 512:(tt + 1) * 512],
                    start=(kc == 0), stop=(kc == NCD - 1)),
                   inc=(c_pe if kc == NCD - 1 else None), waits=w0)
            ev("ve", lambda e, kh=kh, tt=tt, bk=bk: e.tensor_copy(
                k_fm[:, kh, 512 + tt * 512:512 + (tt + 1) * 512], bankf(bk)),
               inc=c_ve, waits=[(c_pe, c_pe.n), (c_gp, PADS)])
            brel(bk, c_ve)
            pj += 1
        kh_done[kh] = c_pe.n
    for tc in range(NTC):
        bk = 2 + pj % 4
        for kc in range(NCD):
            w0 = []
            if kc == 0:
                w0 = [(dWv, WV_OK)] + bw(bk)
            ev("pe", lambda e, tc=tc, kc=kc, bk=bk: e.matmul(
                bankf(bk), h1[:, kc, tc * 128:(tc + 1) * 128], wv_rs[:, kc, :],
                start=(kc == 0), stop=(kc == NCD - 1)),
               inc=(c_pe if kc == NCD - 1 else None), waits=w0)
        ev("ve", lambda e, tc=tc, bk=bk: e.tensor_copy(
            v_tm[:, 4 + tc, :], bankf(bk)),
           inc=c_ve, waits=[(c_pe, c_pe.n), (c_gp, PADS)])
        brel(bk, c_ve)
        pj += 1
    VK_EV = c_ve.n
    h_done = {}
    qev_done = {}
    for h in range(H):
        par = h % 2
        wfree = [(c_pe, h_done[h - 2])] if h >= 2 else [
            (c_pe, kh_done[2 + par])]
        ev("gp", lambda e, h=h, par=par: e.dma_start(
            wblk[:, par, :, :],
            wq[:, h * 128:(h + 1) * 128].rearrange("(c p) n -> p c n", p=128)),
           sc=dWb[par], waits=wfree)
        ld = dWb[par].n
        for tt in range(4):
            bk = 2 + pj % 4
            for kc in range(NCD):
                w0 = []
                if kc == 0:
                    w0 = [(dWb[par], ld)] + bw(bk)
                ev("pe", lambda e, par=par, tt=tt, kc=kc, bk=bk: e.matmul(
                    bankf(bk), wblk[:, par, kc, :],
                    h1[:, kc, tt * 512:(tt + 1) * 512],
                    start=(kc == 0), stop=(kc == NCD - 1)),
                   inc=(c_pe if kc == NCD - 1 else None), waits=w0)
            ev("se", lambda e, h=h, tt=tt, bk=bk: e.activation(
                q_fm[:, h, tt * 512:(tt + 1) * 512], bankf(bk), AF.Identity,
                bias=0.0, scale=QSCALE),
               inc=c_ev, waits=[(c_pe, c_pe.n)])
            brel(bk, c_ev)
            pj += 1
        h_done[h] = c_pe.n
        qev_done[h] = c_ev.n
    Q_EV = c_ev.n



    VR0 = c_vr.n
    VD0 = c_vd.n
    NIT = NQ * H

    def it_qc_h(it):
        return it // H, it % H

    def emit_qk(it):
        qc, h = it_qc_h(it)
        vm = min(qc, 4)
        p = it % 2
        qsl = q_fm[:, h, qc * 128:(qc + 1) * 128]
        ksl = k_fm[:, h // 4, qc * 128:qc * 128 + WINC]
        scA = PSf[:, p * 1024:p * 1024 + 512]
        scB = PSf[:, p * 1024 + 512:p * 1024 + WINC]
        w0 = [(c_exp, it - 1), (c_max, it - 1)]
        if it == 0:
            w0 += [(c_ve, VK_EV), (c_ev, Q_EV), (dCn, MASKS_OK),
                   (c_gp, PADS)]
        ev("pe", lambda e, qsl=qsl, ksl=ksl, scA=scA: e.matmul(
            scA, qsl, ksl[:, 0:512], start=True, stop=False), waits=w0)
        ev("pe", lambda e, qsl=qsl, ksl=ksl, scB=scB: e.matmul(
            scB, qsl, ksl[:, 512:WINC], start=True, stop=False))
        ev("pe", lambda e, vm=vm, scA=scA: e.matmul(
            scA, id_bf[:, :], s_masks[:, vm, 0:512], start=False, stop=True))
        ev("pe", lambda e, vm=vm, scB=scB: e.matmul(
            scB, id_bf[:, :], s_masks[:, vm, 512:WINC], start=False,
            stop=True), inc=c_qk)

    def emit_pt(it):
        p = it % 2
        ptb = PSf[:, 2048:2048 + WINC]
        for j in range(5):
            w1 = []
            if j == 0:
                w1 = [(c_exp, it + 1), (c_vd, VD0 + it + 1), (c_pte, it)]
            ev("pe", lambda e, j=j, p=p, ptb=ptb: e.matmul(
                ptb[:, j * 128:(j + 1) * 128],
                s_p[:, p, j * 128:(j + 1) * 128], s_d[:, p, :],
                start=True, stop=True),
               inc=(c_pt if j == 4 else None), waits=w1)

    def emit_av(it):
        qc, h = it_qc_h(it)
        p = it % 2
        avb = PSf[:, 3072 + p * 512:3072 + p * 512 + 128]
        for j in range(5):
            w1 = []
            if j == 0:
                w1 = [(c_pte, it + 1), (c_ave, it - 1)]
            ev("pe", lambda e, j=j, p=p, qc=qc, kh=h // 4, avb=avb: e.matmul(
                avb, v_tm[:, qc + j, kh * 128:(kh + 1) * 128],
                s_pt[:, p, j * 128:(j + 1) * 128],
                start=(j == 0), stop=(j == 4)),
               inc=(c_av if j == 4 else None), waits=w1)

    def emit_max(it):
        p = it % 2
        scAll = PSf[:, p * 1024:p * 1024 + WINC]
        ev("ve", lambda e, scAll=scAll, p=p: e.reduce_max(
            negmax[:, p:p + 1], scAll, AX.X, negate=True),
           inc=c_max,
           waits=[(c_qk, it + 1)] + ([(dCn, ID_OK)] if it == 0 else []))

    def emit_rd(it):
        p = it % 2
        ev("ve", lambda e, p=p: e.reciprocal(rsum[:, p:p + 1],
                                             esum[:, p:p + 1]),
           inc=c_vr, waits=[(c_exp, it + 1)])
        ev("ve", lambda e, p=p: e.tensor_scalar(
            s_d[:, p, :], id_bf[:, :], rsum[:, p:p + 1], None, ALU.mult),
           inc=c_vd, waits=[(c_vr, VR0 + it + 1), (c_pt, it - 1)])

    def emit_ave(it):
        qc, h = it_qc_h(it)
        p = it % 2
        avb = PSf[:, 3072 + p * 512:3072 + p * 512 + 128]
        ev("ve", lambda e, p=p, h=h, qc=qc, avb=avb: e.tensor_copy(
            attn[:, h, qc * 128:(qc + 1) * 128], avb),
           inc=c_ave, waits=[(c_av, it + 1)])

    def emit_exp(it):
        p = it % 2
        scAll = PSf[:, p * 1024:p * 1024 + WINC]
        ev("se", lambda e, scAll=scAll, p=p: e.activation(
            s_p[:, p, :], scAll, AF.Exp, bias=negmax[:, p:p + 1],
            scale=1.0, accum_out=esum[:, p:p + 1]),
           inc=c_exp, waits=[(c_max, it + 1), (c_pt, it - 1)])

    def emit_pte(it):
        p = it % 2
        ptb = PSf[:, 2048:2048 + WINC]
        ev("se", lambda e, p=p, ptb=ptb: e.copy(s_pt[:, p, :], ptb),
           inc=c_pte, waits=[(c_pt, it + 1), (c_av, it - 1)])

    for m in range(NIT + 4):
        if m < NIT:
            emit_qk(m)
        if 2 <= m <= NIT + 1:
            emit_pt(m - 2)
        if 4 <= m <= NIT + 3:
            emit_av(m - 4)
        if 1 <= m <= NIT:
            emit_rd(m - 1)
        if m < NIT:
            emit_max(m)
        if 4 <= m <= NIT + 3:
            emit_ave(m - 4)
        if 2 <= m <= NIT + 1:
            emit_pte(m - 2)
        if m < NIT:
            emit_exp(m)
    ATT_DONE = c_ave.n

    for i in range(6):
        bank_rel[i] = None

    wo_done = {}
    for dc in range(NCD):
        par = dc % 2
        wfree = [(c_pe, wo_done[dc - 2])] if dc >= 2 else [
            (c_pe, h_done[H - 2 + par]), (dCn, WO16_OK)]
        ev("sy", lambda e, dc=dc, par=par: e.dma_start(
            wblk[:, par, :, :],
            wo16[:, dc * 128:(dc + 1) * 128].rearrange(
                "(c p) n -> p c n", p=128)),
           sc=dWo[par], waits=wfree)
        ld = dWo[par].n
        for tt in range(4):
            bk = (dc * 4 + tt) % 4
            for af in range(NCD):
                w0 = []
                if af == 0:
                    w0 = [(dWo[par], ld)] + bw(bk)
                    if dc == 0 and tt == 0:
                        w0 += [(c_ave, ATT_DONE), (c_exp, 256), (c_max, 256)]
                ev("pe", lambda e, par=par, tt=tt, af=af, bk=bk: e.matmul(
                    bankf(bk), wblk[:, par, af, :],
                    attn[:, af, tt * 512:(tt + 1) * 512],
                    start=(af == 0), stop=(af == NCD - 1)),
                   inc=(c_pe if af == NCD - 1 else None), waits=w0)
            ev("se", lambda e, dc=dc, tt=tt, bk=bk: e.activation(
                contrib[:, dc, tt * 512:(tt + 1) * 512], bankf(bk), AF.Identity,
                bias=0.0, scale=modall[:, 32 + dc:33 + dc]),
               inc=c_wo, waits=[(c_pe, c_pe.n), (c_mod, 1)])
            brel(bk, c_wo)
        wo_done[dc] = c_pe.n
    WO_DONE = c_wo.n

    pre_steps = []
    for m in range(NM):
        pre_steps.append((gw[:, m * 128:(m + 1) * 128].rearrange(
            "(c p) n -> p c n", p=128), g_scr[m]))
    for m in range(NM):
        pre_steps.append((uw[:, m * 128:(m + 1) * 128].rearrange(
            "(c p) n -> p c n", p=128), u_scr[m]))
    for dc in range(NCD):
        for sub in range(3):
            pre_steps.append((
                dw[2048 * sub:2048 * (sub + 1),
                   dc * 128:(dc + 1) * 128].rearrange(
                    "(c p) n -> p c n", p=128),
                d_scr[dc][:, 2048 * sub:2048 * (sub + 1)]))
    st_done = {}
    ld_done = {}
    NPRE = len(pre_steps)
    for si in range(NPRE + 2):
        if si < NPRE:
            psrc = pre_steps[si][0]
            par = si % 3
            lp = si % 2
            w = [st_done[si - 3]] if si - 3 in st_done else []
            ev("gp", lambda e, psrc=psrc, par=par: e.dma_start(
                pre8[:, par, :].rearrange("p (c n) -> p c n", n=128), psrc),
               sc=dPre2[lp], waits=w)
            ld_done[si] = (dPre2[lp], dPre2[lp].n)
        sj = si - 2
        if sj >= 0:
            dst = pre_steps[sj][1]
            par = sj % 3
            lp = sj % 2
            ev("gp", lambda e, dst=dst, par=par: e.dma_start(
                dst.bitcast(U32), pre8[:, par, :].bitcast(U32)),
               sc=dScr2[lp], waits=[ld_done[sj]])
            st_done[sj] = (dScr2[lp], dScr2[lp].n)
    PRE_A, PRE_B = dScr2[0].n, dScr2[1].n



    epi_done = 0
    gup_mm_done = {}
    dn_prev = {}
    for Q in range(4):
        h2tr0 = c_tr.n
        for t4 in range(4):
            tc = Q * 4 + t4
            par = t4 % 2
            sc_ = tc % 16
            rel = [(c_sq, NTC + tc - 1), (c_vxh, NTC + tc - 1)]
            if Q == 0 and t4 < 2:
                rel = [(c_wo, WO_DONE)]
            ev("sy", lambda e, tc=tc, par=par: e.dma_start(
                s_u2[:, par, :], xin[tc * 128:(tc + 1) * 128, :]),
               sc=dX[par], waits=rel)
            nx = dX[par].n
            tr0 = c_tr.n
            for dcg in range(4):
                bk = 4 + dcg
                for d4 in range(4):
                    dc = dcg * 4 + d4
                    w0 = []
                    if d4 == 0:
                        w0 = [(c_wo, WO_DONE), (c_ut, c_ut.n),
                              (c_fve, epi_done)] + bw(4 + dcg)
                    ev("pe", lambda e, dc=dc, d4=d4, tc=tc, dcg=dcg: e.matmul(
                        bankf(4 + dcg)[:, d4 * 128:(d4 + 1) * 128],
                        contrib[:, dc, tc * 128:(tc + 1) * 128], id_bf[:, :],
                        start=(d4 == 0), stop=(d4 == 3)),
                       inc=(c_tr if d4 == 3 else None), waits=w0)
            for dcg in range(4):
                ev("ve", lambda e, dcg=dcg, par=par: e.tensor_tensor(
                    s_u2[:, par, dcg * 512:(dcg + 1) * 512],
                    s_u2[:, par, dcg * 512:(dcg + 1) * 512],
                    bankf(4 + dcg), ALU.add),
                   inc=c_ut, waits=[(c_tr, tr0 + dcg + 1), (dX[par], nx),
                                    (c_vxh, NTC + tc - 1)])
                brel(4 + dcg, c_ut)
            ev("ve", lambda e, sc_=sc_, par=par: e.reduce_sum(
                sumx[:, sc_:sc_ + 1], s_u2[:, par, :], AX.X), sc=c_ve,
               waits=[(c_ut, c_ut.n)])
            ev("se", lambda e, sc_=sc_, par=par: e.activation(
                sq2[:, :], s_u2[:, par, :], AF.Square, bias=0.0, scale=1.0,
                accum_out=ssq[:, sc_:sc_ + 1]),
               sc=c_sq, waits=[(c_ut, c_ut.n)])
            ev("ve", lambda e, sc_=sc_: e.tensor_scalar(
                smean[:, sc_:sc_ + 1], sumx[:, sc_:sc_ + 1], INV_D, None,
                ALU.mult), waits=[(c_sq, c_sq.n)], sc=c_ve)
            ev("ve", lambda e, sc_=sc_: e.tensor_scalar(
                svar[:, sc_:sc_ + 1], ssq[:, sc_:sc_ + 1], INV_D, EPS,
                ALU.mult, ALU.add), sc=c_ve)
            ev("ve", lambda e, sc_=sc_: e.tensor_tensor(
                tmpa[:, sc_:sc_ + 1], smean[:, sc_:sc_ + 1],
                smean[:, sc_:sc_ + 1], ALU.mult), sc=c_ve)
            ev("ve", lambda e, sc_=sc_: e.tensor_tensor(
                svar[:, sc_:sc_ + 1], svar[:, sc_:sc_ + 1],
                tmpa[:, sc_:sc_ + 1], ALU.subtract), sc=c_ve, inc=c_vst)
            ev("se", lambda e, sc_=sc_: e.sqrt(
                ssd[:, sc_:sc_ + 1], svar[:, sc_:sc_ + 1]),
               inc=c_ssd, waits=[(c_vst, c_vst.n)])
            ev("ve", lambda e, sc_=sc_: e.reciprocal(
                srstd[:, sc_:sc_ + 1], ssd[:, sc_:sc_ + 1]),
               inc=c_vr, waits=[(c_ssd, c_ssd.n)])
            ev("ve", lambda e, sc_=sc_, par=par, t4=t4: e.tensor_scalar(
                s_xh2[:, t4, :], s_u2[:, par, :], smean[:, sc_:sc_ + 1],
                srstd[:, sc_:sc_ + 1], ALU.subtract, ALU.mult),
               inc=c_vxh, waits=[(c_vr, c_vr.n), (c_tr, h2tr0)])
        XH4 = c_vxh.n
        for dc in range(NCD):
            bk = dc % 2
            for t4 in range(4):
                w0 = []
                if t4 == 0:
                    w0 = [(c_vxh, XH4)] + bw(6 + bk)
                ev("pe", lambda e, t4=t4, dc=dc, bk=bk: e.matmul(
                    bankf(6 + bk)[:, t4 * 128:(t4 + 1) * 128],
                    s_xh2[:, t4, dc * 128:(dc + 1) * 128], id_bf[:, :],
                    start=(t4 == 0), stop=(t4 == 3)),
                   inc=(c_tr if t4 == 3 else None), waits=w0)
            ev("se", lambda e, dc=dc, bk=bk: e.activation(
                h2[:, dc, :], bankf(6 + bk), AF.Identity,
                bias=modall[:, 48 + dc:49 + dc],
                scale=modall[:, 64 + dc:65 + dc]),
               inc=c_h2, waits=[(c_tr, c_tr.n)])
            brel(6 + bk, c_h2)
        H2Q = c_h2.n
        stt_done = {}
        for m in range(NM):
            par = m % 2
            gidx = Q * NM + m
            w0 = [(dScr2[0], PRE_A), (dScr2[1], PRE_B)]
            if Q == 0 and m < 2:
                w0 += [(c_wo, WO_DONE)]
            if gidx >= 2:
                w0 += [(c_fpe, gup_mm_done[gidx - 2])]
            ev("sy", lambda e, m=m, par=par: e.dma_start(g8[:, par, :],
                                                         g_scr[m]),
               sc=dG[par], waits=w0)
            ldg = dG[par].n
            ev("sy", lambda e, m=m, par=par: e.dma_start(u8w[:, par, :],
                                                         u_scr[m]),
               sc=dU[par])
            ldu = dU[par].n
            for kc in range(NCD):
                w1 = []
                if kc == 0:
                    w1 = [(dG[par], ldg), (c_h2, H2Q)] + bw(0 + par)
                ev("pe", lambda e, par=par, kc=kc: e.matmul(
                    bankf(0 + par),
                    g8[:, par, :].rearrange("p (c n) -> p c n", c=16)[
                        :, kc, :].bitcast(F8),
                    h2[:, kc, :], start=(kc == 0), stop=(kc == NCD - 1)),
                   waits=w1)
            for kc in range(NCD):
                w1 = []
                if kc == 0:
                    w1 = [(dU[par], ldu)] + bw(2 + par)
                ev("pe", lambda e, par=par, kc=kc: e.matmul(
                    bankf(2 + par),
                    u8w[:, par, :].rearrange("p (c n) -> p c n", c=16)[
                        :, kc, :].bitcast(F8),
                    h2[:, kc, :], start=(kc == 0), stop=(kc == NCD - 1)),
                   inc=(c_fpe if kc == NCD - 1 else None), waits=w1)
            gup_mm_done[gidx] = c_fpe.n
            w2 = [(c_fpe, c_fpe.n), (c_mod, 1)]
            if m >= 2:
                w2 += [(c_fve, stt_done[m - 2])]
            elif Q > 0:
                w2 += [(c_fve, epi_done)]
            ev("se", lambda e, m=m, par=par: e.activation(
                sy2[:, par, :], bankf(0 + par), AF.Silu, bias=0.0,
                scale=gsus[:, m:m + 1]),
               inc=c_fse, waits=w2)
            brel(0 + par, c_fse)
            ev("ve", lambda e, m=m, par=par: e.scalar_tensor_tensor(
                fuse[:, m, :], bankf(2 + par), gsus[:, 48 + m:49 + m],
                sy2[:, par, :], ALU.mult, ALU.mult),
               inc=c_fve,
               waits=[(c_fse, c_fse.n)]
               + ([(c_ev, 4)] if (Q == 0 and m == 0) else []))
            brel(2 + par, c_fve)
            stt_done[m] = c_fve.n
        FUSEQ = c_fve.n
        GUP_EV = c_fse.n
        dn_mm_done = {}
        yev_done = {}
        stt_y = {}
        trg_after = {}

        def emit_downmm(dc):
            par = dc % 2
            didx = Q * NCD + dc
            w0 = []
            if Q == 0 and dc < 2:
                w0 = [(c_av, 256), (c_pt, 256), (c_qk, 256),
                      (dScr2[0], PRE_A), (dScr2[1], PRE_B)]
            elif dc < 2:
                w0 = [(c_fpe, dn_prev[14 + dc])]
            else:
                w0 = [(c_fpe, dn_mm_done[dc - 2])]
            ev("sy", lambda e, dc=dc, par=par: e.dma_start(
                d8[:, par, :], d_scr[dc]),
               sc=dD8[par], waits=w0)
            ldd = dD8[par].n
            dbk = dc % 2
            for m_ in range(NM):
                w1 = []
                if m_ == 0:
                    w1 = [(dD8[par], ldd), (c_fve, FUSEQ)] + bw(dbk)
                ev("pe", lambda e, par=par, m=m_, dbk=dbk: e.matmul(
                    bankf(dbk),
                    d8[:, par, :].rearrange("p (a n) -> p a n", a=NM)[
                        :, m, :].bitcast(F8),
                    fuse[:, m, :], start=(m == 0), stop=(m == NM - 1)),
                   inc=(c_fpe if m_ == NM - 1 else None), waits=w1)
            dn_mm_done[dc] = c_fpe.n
            w2 = [(c_fpe, c_fpe.n)]
            if Q == 0 and dc == 0:
                w2 += [(c_mod, 1)]
            if dc >= 2:
                w2 += [(c_tr, yev_done[dc - 2])]
            ev("ve", lambda e, par=par, dc=dc, dbk=dbk, Q=Q:
               e.scalar_tensor_tensor(
                   sy2[:, par, :], bankf(dbk), gbds[:, dc:dc + 1],
                   contrib[:, dc, Q * 512:(Q + 1) * 512],
                   ALU.mult, ALU.add),
               inc=c_fve, waits=w2)
            brel(dbk, c_fve)
            stt_y[dc] = c_fve.n

        def emit_transp(dc):
            par = dc % 2
            d4 = dc % 4
            for t4 in range(4):
                w3 = []
                if t4 == 0:
                    w3 = [(c_fve, stt_y[dc])]
                    if d4 == 0:
                        w3 += bw(4) + bw(5) + bw(6) + bw(7)
                ev("pe", lambda e, t4=t4, par=par, d4=d4: e.matmul(
                    bankf(4 + t4)[:, d4 * 128:(d4 + 1) * 128],
                    sy2[:, par, t4 * 128:(t4 + 1) * 128], id_bf[:, :],
                    start=True, stop=True),
                   inc=(c_tr if t4 == 3 else None), waits=w3)
            yev_done[dc] = c_tr.n

        def emit_xe(dcg):
            guard = FUSEQ if dcg < 2 else epi_add_after[dcg - 2]
            for t4 in range(4):
                tc = Q * 4 + t4
                ev("gp", lambda e, tc=tc, t4=t4, dcg=dcg: e.dma_start(
                    s_xe2[:, t4, dcg % 2, :],
                    xin[tc * 128:(tc + 1) * 128, dcg * 512:(dcg + 1) * 512]),
                   sc=dXE, waits=([(c_fve, guard)] if t4 == 0 else []))
            xe_cnt[dcg] = dXE.n

        def emit_epilogue(dcg):
            TRG = trg_after[dcg]
            for t4 in range(4):
                tc = Q * 4 + t4
                g = Q * 4 + dcg
                ev("ve", lambda e, t4=t4, dcg=dcg: e.tensor_tensor(
                    x3p[:, t4, :], s_xe2[:, t4, dcg % 2, :],
                    bankf(4 + t4), ALU.add),
                   inc=c_fve,
                   waits=[(c_tr, TRG), (dXE, xe_cnt[dcg]),
                          (dOL[t4], 16 * g)])
                brel(4 + t4, c_fve)
                ev("gp", lambda e, tc=tc, dcg=dcg, t4=t4: e.dma_start(
                    yout[tc * 128:(tc + 1) * 128,
                         dcg * 512:(dcg + 1) * 512], x3p[:, t4, :]),
                   sc=dOL[t4], waits=[(c_fve, c_fve.n)])
            epi_add_after[dcg] = c_fve.n

        xe_cnt = {}
        epi_add_after = {}
        for dc in range(NCD + 1):
            if dc < NCD:
                if dc % 4 == 0:
                    emit_xe(dc // 4)
                emit_downmm(dc)
            if dc >= 1:
                emit_transp(dc - 1)
                if (dc - 1) % 4 == 3:
                    trg_after[(dc - 1) // 4] = c_tr.n
                    emit_epilogue((dc - 1) // 4)
                dn_prev = dn_mm_done
        epi_done = c_fve.n

    drain = [(s, s.n) for s in dOL] + [(dO, dO.n)]
    ev("sy", lambda e: e.nop(), waits=drain)
    ev("gp", lambda e: e.nop(), waits=drain)

    for name in DBG:
        tap = {
            "h1": (h1, c_ev, H1_EV),
            "q": (q_fm, c_ve, QKV_EV),
            "attn": (attn, c_ave, ATT_DONE),
            "contrib": (contrib, c_wo, WO_DONE),
            "modall": (modall, c_mod, 1),
            "kf": (k_fm, c_ve, QKV_EV),
            "vt": (v_tm, c_ve, QKV_EV),
        }[name]
        tsr, ctr, cnt = tap
        flat = tsr.ap()
        if len(flat.shape) == 3:
            flat = flat.rearrange("p a b -> p (a b)")
        out_t = nc.dram_tensor("dbg_" + name, list(flat.shape), F32,
                               kind="ExternalOutput").ap()
        ev("gp", lambda e, flat=flat, out_t=out_t: e.dma_start(out_t, flat),
           sc=dO, waits=[(ctr, cnt)])
        ev("gp", lambda e, n=dO.n: e.nop(), waits=[(dO, dO.n)])

    _assemble()
    stack.close()
    return nc


def _host_consts():
    i = np.arange(128)[:, None]
    j = np.arange(WINC)[None, :]
    band = (j >= i + 1) & (j <= i + 512)
    masks = np.empty((5, 128, WINC), np.float32)
    for v in range(5):
        cutoff = 512 - v * 128 if v < 4 else 0
        allowed = band & (j >= cutoff)
        masks[v] = np.where(allowed, 0.0, -30000.0).astype(np.float32)
    return masks, np.eye(128, dtype=np.float32)


def _pad_vec(v, rows):
    out = np.zeros((128, 128), np.float32)
    out[:rows] = np.asarray(v, np.float32).reshape(rows, 128)
    return out


def kernel(**inputs):
    f32 = lambda a: np.ascontiguousarray(np.asarray(a, np.float32))
    x = f32(inputs["L0_attn_x"])
    masks, ident = _host_consts()
    common = {
        "wq": f32(inputs["L0_attn_Wq"]),
        "wk": f32(inputs["L0_attn_Wk"]),
        "wv": f32(inputs["L0_attn_Wv"]),
        "wo": f32(inputs["L0_attn_Wo"]),
        "wmod": f32(inputs["L0_wmod"]),
        "gw": f32(inputs["L0_ffn_gate_w"]),
        "uw": f32(inputs["L0_ffn_up_w"]),
        "dw": f32(inputs["L0_ffn_down_w"]),
        "gs_p": _pad_vec(inputs["L0_ffn_gate_scale"], 48),
        "us_p": _pad_vec(inputs["L0_ffn_up_scale"], 48),
        "ds_p": _pad_vec(inputs["L0_ffn_down_scale"], 16),
        "ident": ident,
        "masks": masks,
    }
    c_all = f32(inputs["L0_c"])
    if "nc" not in _CACHE:
        _CACHE["nc"] = build()
    nc = _CACHE["nc"]
    in_maps = []
    for b in range(8):
        m = dict(common)
        m["x"] = np.ascontiguousarray(x[b])
        m["c_p"] = _pad_vec(c_all[b], 16)
        in_maps.append(m)
    res = run_bass_kernel_spmd(nc, in_maps, core_ids=list(range(8)))
    return np.stack([res.results[b]["y"] for b in range(8)], axis=0)


# version 18
# speedup vs baseline: 1.0879x; 1.0879x over previous
import math
from contextlib import ExitStack

import numpy as np

import concourse.bass as bass
import concourse.mybir as mybir
from concourse.bass_utils import run_bass_kernel_spmd

F32 = mybir.dt.float32
BF16 = mybir.dt.bfloat16
F8 = mybir.dt.float8e4
U8 = mybir.dt.uint8
U32 = mybir.dt.uint32
AF = mybir.ActivationFunctionType
ALU = mybir.AluOpType
AX = mybir.AxisListType

D = 2048
T = 2048
NCD = 16
NTC = 16
H, HK, HD = 16, 4, 128
NQ = 16
WINC = 640
I = 6144
NM = 48
EPS = 1e-5
QSCALE = 1.0 / math.sqrt(HD)
INV_D = 1.0 / D

DBG = []
_CACHE = {}


class Ctr:
    def __init__(self, stack, nc, name, step=1):
        self.h = stack.enter_context(nc.semaphore(name))
        self.step = step
        self.n = 0

    def bump(self):
        self.n += self.step
        return self.n


def build():
    nc = bass.Bass("TRN2", target_bir_lowering=False)
    stack = ExitStack()

    xin = nc.dram_tensor("x", [T, D], F32, kind="ExternalInput").ap()
    wq = nc.dram_tensor("wq", [D, H * HD], F32, kind="ExternalInput").ap()
    wk = nc.dram_tensor("wk", [D, HK * HD], F32, kind="ExternalInput").ap()
    wv = nc.dram_tensor("wv", [D, HK * HD], F32, kind="ExternalInput").ap()
    wo = nc.dram_tensor("wo", [H * HD, D], F32, kind="ExternalInput").ap()
    wmod = nc.dram_tensor("wmod", [D, 6 * D], F32, kind="ExternalInput").ap()
    gw = nc.dram_tensor("gw", [D, I], F32, kind="ExternalInput").ap()
    uw = nc.dram_tensor("uw", [D, I], F32, kind="ExternalInput").ap()
    dw = nc.dram_tensor("dw", [I, D], F32, kind="ExternalInput").ap()
    c_p = nc.dram_tensor("c_p", [128, 128], F32, kind="ExternalInput").ap()
    gs_p = nc.dram_tensor("gs_p", [128, 128], F32, kind="ExternalInput").ap()
    us_p = nc.dram_tensor("us_p", [128, 128], F32, kind="ExternalInput").ap()
    ds_p = nc.dram_tensor("ds_p", [128, 128], F32, kind="ExternalInput").ap()
    ident = nc.dram_tensor("ident", [128, 128], F32, kind="ExternalInput").ap()
    masks = nc.dram_tensor("masks", [5, 128, WINC], F32, kind="ExternalInput").ap()
    yout = nc.dram_tensor("y", [T, D], F32, kind="ExternalOutput").ap()

    mod_scr = nc.dram_tensor("mod_scr", [1, 6 * D], F32, kind="Internal").ap()
    g_scr = nc.dram_tensor("g_scr", [NM, 128, 2048], U8, kind="Internal").ap()
    u_scr = nc.dram_tensor("u_scr", [NM, 128, 2048], U8, kind="Internal").ap()
    d_scr = nc.dram_tensor("d_scr", [NCD, 128, 6144], U8, kind="Internal").ap()
    wo16 = nc.dram_tensor("wo16", [H * HD, D], BF16, kind="Internal").ap()

    BASE = 16640
    LIMIT = 229200
    cur = [BASE]

    OFFS = {}

    def sb(name, shape, dtype, off=None, align=64):
        bpp = int(np.prod(shape[1:])) * mybir.dt.size(dtype)
        if off is None:
            cur[0] = (cur[0] + align - 1) // align * align
            off = cur[0]
            cur[0] += bpp
        assert off + bpp <= LIMIT, (name, off, bpp)
        OFFS[name] = off
        return nc.alloc_sbuf_tensor_at(name, shape, dtype, offset=off)

    id_bf = sb("id_bf", [128, 128], BF16)
    id32 = sb("id32", [128, 128], F32)
    modall = sb("modall", [128, 96], F32)
    gsus = sb("gsus", [128, 96], F32)
    dsq = sb("dsq", [128, 16], F32)
    gbds = sb("gbds", [128, 16], F32)
    csi = sb("csi", [128, 16], BF16)
    tmpa = sb("tmpa", [128, 16], F32)
    tmpb = sb("tmpb", [128, 16], F32)
    tmpc = sb("tmpc", [128, 16], F32)
    sumx = sb("sumx", [128, 16], F32)
    ssq = sb("ssq", [128, 16], F32)
    smean = sb("smean", [128, 16], F32)
    svar = sb("svar", [128, 16], F32)
    ssd = sb("ssd", [128, 16], F32)
    srstd = sb("srstd", [128, 16], F32)
    negmax = sb("negmax", [128, 2], F32)
    esum = sb("esum", [128, 2], F32)
    rsum = sb("rsum", [128, 2], F32)
    vecs = sb("vecs", [128, 128], F32)
    x3p = sb("x3p", [128, 4, 512], F32)
    sy2 = sb("sy2", [128, 2, 512], BF16)
    PERS_END = cur[0]

    O_ATN = (PERS_END + 63) // 64 * 64
    s_masks = sb("s_masks", [128, 5, WINC], BF16, off=O_ATN)
    s_p = sb("s_p", [128, 2, WINC], BF16, off=O_ATN + 6400)
    s_pt = sb("s_pt", [128, 2, WINC], BF16, off=O_ATN + 6400 + 2560)
    s_d = sb("s_d", [128, 2, 128], BF16, off=O_ATN + 6400 + 5120)
    d8 = sb("d8", [128, 2, 6144], U8, off=O_ATN)
    O_WBUF = O_ATN + 12544

    wblk = sb("wblk", [128, 2, 16, 128], BF16, off=O_WBUF)
    g8 = sb("g8", [128, 2, 2048], U8, off=O_WBUF)
    u8w = sb("u8w", [128, 2, 2048], U8, off=O_WBUF + 4096)
    O_KV = O_WBUF + 8192

    k_fm = sb("k_fm", [128, HK, T + 512], BF16, off=O_KV)
    v_tm = sb("v_tm", [128, NTC + 4, 512], BF16, off=O_KV + 20480)
    s_u2 = sb("s_u2", [128, 2, D], F32, off=O_KV)
    s_xh2 = sb("s_xh2", [128, 4, D], BF16, off=O_KV + 16384)
    sq2 = sb("sq2", [128, D], F32, off=O_KV + 32768)
    xhq2 = sb("xhq2", [128, 4, D], BF16, off=O_KV + 16384)
    xhq3 = sb("xhq3", [128, 4, D], BF16, off=O_KV)
    O_A = O_KV + 41088

    wmodb = sb("wmodb", [128, 4, 3072], BF16, off=O_A)
    h1 = sb("h1", [128, NCD, T], BF16, off=O_A)
    attn = sb("attn", [128, H, T], BF16, off=O_A)
    fuse = sb("fuse", [128, NM, 512], BF16, off=O_A)
    h2 = sb("h2", [128, NCD, 512], BF16, off=O_A + 49152)
    s_xe2 = sb("s_xe2", [128, 4, 2, 512], F32, off=O_A + 49152)
    O_B = O_A + 65536

    s_x1 = sb("s_x1", [128, 2, D], F32, off=O_B)
    xhq = sb("xhq", [128, 4, D], BF16, off=O_B + 16384)
    sq1 = sb("sq1", [128, D], F32, off=O_B + 32768)
    modtmp = sb("modtmp", [1, 3072], F32, off=O_B + 40960)
    wv_rs = sb("wv_rs", [128, 16, 512], BF16, off=O_B + 16384)
    q_fm = sb("q_fm", [128, H, T], BF16, off=O_B)
    contrib = sb("contrib", [128, NCD, T], BF16, off=O_B)
    O_TOP = O_B + 65536
    pre8 = sb("pre8", [128, 3, 2048], F8, off=O_TOP)
    assert O_TOP + 6144 <= LIMIT

    PSf = nc.alloc_psum_tensor("PSf", [128, 4096], F32)

    def bankf(i, w=512):
        return PSf[:, 512 * i:512 * i + w]

    OPS = {k: [] for k in ("pe", "ve", "se", "gp", "sy")}

    def C(name, step=1):
        return Ctr(stack, nc, name, step)

    dCn = C("dCn", 16)
    dWm = [C("dWm0", 16), C("dWm1", 16), C("dWm2", 16), C("dWm3", 16)]
    dWb = [C("dWb0", 16), C("dWb1", 16)]
    dPre = C("dPre", 16)
    dScr = C("dScr", 16)
    dMT = C("dMT", 16)
    dX = [C("dX0", 16), C("dX1", 16)]
    dXE = C("dXE", 16)
    dG = [C("dG0", 16), C("dG1", 16)]
    dU = [C("dU0", 16), C("dU1", 16)]
    dD8 = [C("dD80", 16), C("dD81", 16)]
    dWo = [C("dWo0", 16), C("dWo1", 16)]
    dPre2 = [C("dPreA", 16), C("dPreB", 16)]
    dScr2 = [C("dScrA", 16), C("dScrB", 16)]
    dO = C("dO", 16)
    dOL = [C("dO0", 16), C("dO1", 16), C("dO2", 16), C("dO3", 16)]
    dWv = C("dWv", 16)

    c_pe = C("c_pe")
    c_ve = C("c_ve")
    c_se = C("c_se")
    c_gp = C("c_gp")
    c_mod = C("c_mod")
    c_sgn = C("c_sgn")
    c_sq = C("c_sq")
    c_ssd = C("c_ssd")
    c_vst = C("c_vst")
    c_vr = C("c_vr")
    c_vxh = C("c_vxh")
    c_tr = C("c_tr")
    c_ev = C("c_ev")
    c_lau = C("c_lau")
    c_qk = C("c_qk")
    c_max = C("c_max")
    c_exp = C("c_exp")
    c_vd = C("c_vd")
    c_pt = C("c_pt")
    c_pte = C("c_pte")
    c_av = C("c_av")
    c_ave = C("c_ave")
    c_wo = C("c_wo")
    c_fpe = C("c_fpe")
    c_fse = C("c_fse")
    c_fve = C("c_fve")
    c_h2 = C("c_h2")
    c_ut = C("c_ut")

    def ev(eng, fn, inc=None, waits=(), sc=None):
        waits = list(waits)
        if sc is not None:
            waits.append((sc, sc.n))
            if inc is None:
                inc = sc
        w = tuple((c.h, v) for (c, v) in waits if v > 0)
        tgt = None
        if inc is not None:
            inc.bump()
            tgt = (inc.h, inc.step)

        def run(e, fn=fn, w=w, tgt=tgt):
            for hh, vv in w:
                e.wait_ge(hh, vv)
            r = fn(e)
            if tgt is not None:
                r.then_inc(tgt[0], tgt[1])
        OPS[eng].append(run)

    bank_rel = {i: None for i in range(8)}

    def bw(i):
        r = bank_rel[i]
        return [] if r is None else [(r[0], r[1])]

    def brel(i, ctr):
        bank_rel[i] = (ctr, ctr.n)

    def _assemble():
        with nc.Block() as block:
            @block.sync
            def _(e):
                for f in OPS["sy"]:
                    f(e)

            @block.gpsimd
            def _(e):
                for f in OPS["gp"]:
                    f(e)

            @block.tensor
            def _(e):
                for f in OPS["pe"]:
                    f(e)

            @block.vector
            def _(e):
                for f in OPS["ve"]:
                    f(e)

            @block.scalar
            def _(e):
                for f in OPS["se"]:
                    f(e)

    ev("gp", lambda e: e.dma_start(id_bf[:, :], ident), sc=dCn)
    ev("gp", lambda e: e.dma_start(id32[:, :], ident), sc=dCn)
    ID_OK = dCn.n
    for vi, src in enumerate((c_p, gs_p, us_p, ds_p)):
        ev("gp", lambda e, s=src: e.dma_start(vecs[:, :], s),
           sc=dCn, waits=[(c_pe, vi)])
        ev("pe", lambda e: e.matmul(bankf(5, 128), vecs[:, :], id32[:, :],
                                    start=True, stop=True),
           inc=c_pe, waits=[(dCn, dCn.n)] + bw(5))
        if vi == 0:
            ev("se", lambda e: e.activation(csi[:, :], bankf(5, 16), AF.Silu,
                                            bias=0.0, scale=1.0),
               inc=c_ev, waits=[(c_pe, c_pe.n)])
            brel(5, c_ev)
        elif vi == 1:
            ev("se", lambda e: e.copy(gsus[:, 0:48], bankf(5, 48)),
               inc=c_ev, waits=[(c_pe, c_pe.n)])
            brel(5, c_ev)
        elif vi == 2:
            ev("se", lambda e: e.copy(gsus[:, 48:96], bankf(5, 48)),
               inc=c_ev, waits=[(c_pe, c_pe.n)])
            brel(5, c_ev)
        else:
            ev("se", lambda e: e.copy(dsq[:, :], bankf(5, 16)),
               inc=c_ev, waits=[(c_pe, c_pe.n)])
            brel(5, c_ev)
    VEC_EV = c_ev.n

    XHB = [xhq, xhq2, xhq3, xhq2]
    ln_tr_done = [0] * 9
    xld = {}
    xh_done = {}

    def emit_xload(tc):
        par = tc % 2
        rel = [(c_sq, tc - 1), (c_vxh, tc - 1)]
        ev("sy", lambda e, tc=tc, par=par: e.dma_start(
            s_x1[:, par, :], xin[tc * 128:(tc + 1) * 128, :]),
           sc=dX[par], waits=rel)
        xld[tc] = (dX[par], dX[par].n)

    def emit_stats(tc):
        par = tc % 2
        nx = xld[tc]
        ev("ve", lambda e, tc=tc, par=par: e.reduce_sum(
            sumx[:, tc:tc + 1], s_x1[:, par, :], AX.X),
           waits=[nx], sc=c_ve)
        ev("se", lambda e, tc=tc, par=par: e.activation(
            sq1[:, :], s_x1[:, par, :], AF.Square, bias=0.0, scale=1.0,
            accum_out=ssq[:, tc:tc + 1]),
           sc=c_sq, waits=[nx])
        ev("ve", lambda e, tc=tc: e.tensor_scalar(
            smean[:, tc:tc + 1], sumx[:, tc:tc + 1], INV_D, None, ALU.mult),
           waits=[(c_sq, tc + 1)], sc=c_ve)
        ev("ve", lambda e, tc=tc: e.tensor_scalar(
            svar[:, tc:tc + 1], ssq[:, tc:tc + 1], INV_D, EPS, ALU.mult,
            ALU.add), sc=c_ve)
        ev("ve", lambda e, tc=tc: e.tensor_tensor(
            tmpa[:, tc:tc + 1], smean[:, tc:tc + 1], smean[:, tc:tc + 1],
            ALU.mult), sc=c_ve)
        ev("ve", lambda e, tc=tc: e.tensor_tensor(
            svar[:, tc:tc + 1], svar[:, tc:tc + 1], tmpa[:, tc:tc + 1],
            ALU.subtract), sc=c_ve, inc=c_vst)
        ev("se", lambda e, tc=tc: e.sqrt(ssd[:, tc:tc + 1], svar[:, tc:tc + 1]),
           inc=c_ssd, waits=[(c_vst, c_vst.n)])
        ev("ve", lambda e, tc=tc: e.reciprocal(srstd[:, tc:tc + 1],
                                               ssd[:, tc:tc + 1]),
           inc=c_vr, waits=[(c_ssd, c_ssd.n)])

    def emit_xhat(tc, extra=()):
        par = tc % 2
        q4 = tc % 4
        qq = tc // 4
        buf = XHB[qq]
        ev("ve", lambda e, tc=tc, par=par, q4=q4, buf=buf: e.tensor_scalar(
            buf[:, q4, :], s_x1[:, par, :], smean[:, tc:tc + 1],
            srstd[:, tc:tc + 1], ALU.subtract, ALU.mult),
           inc=c_vxh, waits=[(c_vr, c_vr.n)] + list(extra))
        if q4 == 3:
            xh_done[qq] = c_vxh.n

    for tc in range(14):
        emit_xload(tc)
        emit_stats(tc)
        if tc < 12:
            emit_xhat(tc)

    wk_ld = {}
    for kh in (0, 1):
        ev("gp", lambda e, kh=kh: e.dma_start(
            wblk[:, kh, :, :],
            wk[:, kh * 128:(kh + 1) * 128].rearrange("(c p) n -> p c n",
                                                     p=128)),
           sc=dWb[kh])
        wk_ld[kh] = dWb[kh].n

    chunk_done = {}
    for g in range(4):
        for kc in range(NCD):
            par = kc % 4
            idx = g * NCD + kc
            wfree = [(c_pe, chunk_done[idx - 4])] if idx >= 4 else []
            ev("gp", lambda e, g=g, kc=kc, par=par: e.dma_start(
                wmodb[:, par, :],
                wmod[kc * 128:(kc + 1) * 128, g * 3072:(g + 1) * 3072]),
               sc=dWm[par], waits=wfree)
            ld = dWm[par].n
            for nt in range(6):
                w0 = []
                if nt == 0:
                    w0 = [(dWm[par], ld), (c_ev, VEC_EV)]
                    if kc == 0:
                        w0 += bw(0) + bw(1) + bw(2) + bw(3) + bw(4) + bw(5)
                ev("pe", lambda e, kc=kc, par=par, nt=nt: e.matmul(
                    PSf[0:1, nt * 512:(nt + 1) * 512], csi[:, kc:kc + 1],
                    wmodb[:, par, nt * 512:(nt + 1) * 512],
                    start=(kc == 0), stop=(kc == NCD - 1)),
                   inc=(c_pe if nt == 5 else None), waits=w0)
            chunk_done[idx] = c_pe.n
        ev("ve", lambda e: e.tensor_copy(modtmp[:, :], PSf[0:1, 0:3072]),
           inc=c_ve, waits=[(c_pe, c_pe.n), (dMT, dMT.n)])
        ev("gp", lambda e, g=g: e.dma_start(
            mod_scr[:, g * 3072:(g + 1) * 3072], modtmp[:, :]),
           sc=dMT, waits=[(c_ve, c_ve.n)])
        for i in range(6):
            brel(i, c_ve)
    ev("gp", lambda e: e.dma_start(
        vecs[0:96, :], mod_scr.rearrange("a (r n) -> (a r) n", n=128)),
       sc=dMT, waits=[(c_pe, c_pe.n)])
    ev("pe", lambda e: e.matmul(bankf(5, 128), vecs[:, :], id32[:, :],
                                start=True, stop=True),
       inc=c_pe, waits=[(dMT, dMT.n)] + bw(5))
    ev("ve", lambda e: e.tensor_copy(modall[:, :], bankf(5, 96)),
       inc=c_ve, waits=[(c_pe, c_pe.n)])
    brel(5, c_ve)
    MODCP = c_ve.n
    ev("ve", lambda e: e.tensor_scalar(modall[:, 16:32], modall[:, 16:32],
                                       1.0, None, ALU.add), sc=c_ve)
    ev("ve", lambda e: e.tensor_scalar(modall[:, 64:80], modall[:, 64:80],
                                       1.0, None, ALU.add), sc=c_ve)
    for b_ in (32, 80):
        ev("se", lambda e, b=b_: e.activation(
            tmpb[:, :], modall[:, b:b + 16], AF.Sign, bias=0.0, scale=1.0),
           inc=c_sgn, waits=[(c_ve, c_ve.n)])
        ev("ve", lambda e: e.tensor_tensor(tmpc[:, :], tmpb[:, :], tmpb[:, :],
                                           ALU.mult), sc=c_ve,
           waits=[(c_sgn, c_sgn.n)])
        ev("ve", lambda e: e.scalar_tensor_tensor(
            tmpb[:, :], tmpb[:, :], 1.0, tmpc[:, :], ALU.add, ALU.subtract),
           sc=c_ve)
        ev("ve", lambda e, b=b_: e.tensor_tensor(
            tmpa[:, :], modall[:, b:b + 16], tmpb[:, :], ALU.mult), sc=c_ve)
        ev("ve", lambda e: e.tensor_scalar(
            tmpa[:, :], tmpa[:, :], 0.25, None, ALU.max), sc=c_ve)
        ev("ve", lambda e, b=b_: e.tensor_tensor(
            modall[:, b:b + 16], tmpa[:, :], tmpb[:, :], ALU.mult), sc=c_ve)
    ev("ve", lambda e: e.tensor_tensor(gbds[:, :], modall[:, 80:96], dsq[:, :],
                                       ALU.mult), sc=c_ve, inc=c_mod)

    ev("gp", lambda e: e.dma_start(s_masks[:, :, :],
                                   masks.rearrange("v p n -> p v n")),
       sc=dCn)
    MASKS_OK = dCn.n

    def emit_trq(qq):
        buf = XHB[qq]
        for dc in range(NCD):
            bk = dc % 2
            for t4 in range(4):
                w0 = []
                if t4 == 0:
                    w0 = [(c_vxh, xh_done[qq]), (dCn, ID_OK)] + bw(6 + bk)
                ev("pe", lambda e, t4=t4, dc=dc, bk=bk, buf=buf: e.matmul(
                    bankf(6 + bk)[:, t4 * 128:(t4 + 1) * 128],
                    buf[:, t4, dc * 128:(dc + 1) * 128], id_bf[:, :],
                    start=(t4 == 0), stop=(t4 == 3)),
                   inc=(c_tr if t4 == 3 else None), waits=w0)
            ev("se", lambda e, dc=dc, bk=bk, qq=qq: e.activation(
                h1[:, dc, qq * 512:(qq + 1) * 512], bankf(6 + bk),
                AF.Identity, bias=modall[:, dc:dc + 1],
                scale=modall[:, 16 + dc:17 + dc]),
               inc=c_ev, waits=[(c_tr, c_tr.n), (c_mod, 1)])
            brel(6 + bk, c_ev)
        ln_tr_done[qq + 1] = c_tr.n

    emit_trq(0)
    emit_trq(1)
    TRQ1 = ln_tr_done[2]
    emit_xhat(12, extra=[(c_tr, TRQ1)])
    emit_xhat(13, extra=[(c_tr, TRQ1)])
    emit_xload(14)
    emit_stats(14)
    emit_xhat(14, extra=[(c_tr, TRQ1)])
    emit_xload(15)
    emit_stats(15)
    emit_xhat(15, extra=[(c_tr, TRQ1)])
    emit_trq(2)
    emit_trq(3)
    H1_EV = c_ev.n
    LN1_TR = c_tr.n

    ev("gp", lambda e: e.memset(k_fm[:, :, 0:512], 0.0), inc=c_gp,
       waits=[(c_tr, LN1_TR)])
    ev("gp", lambda e: e.memset(v_tm[:, 0:4, :], 0.0), inc=c_gp)
    PADS = c_gp.n

    ev("gp", lambda e: e.dma_start(
        wv_rs[:, :, :], wv.rearrange("(c p) n -> p c n", p=128)),
       sc=dWv, waits=[(c_tr, ln_tr_done[1])])
    WV_OK = dWv.n
    ev("gp", lambda e: e.dma_start(wo16, wo), sc=dCn)
    WO16_OK = dCn.n

    pj = 0
    kh_done = {}
    for kh in range(HK):
        par = kh % 2
        if kh < 2:
            ld = wk_ld[kh]
        else:
            ev("gp", lambda e, kh=kh, par=par: e.dma_start(
                wblk[:, par, :, :],
                wk[:, kh * 128:(kh + 1) * 128].rearrange("(c p) n -> p c n",
                                                         p=128)),
               sc=dWb[par], waits=[(c_pe, kh_done[kh - 2])])
            ld = dWb[par].n
        for tt in range(4):
            bk = 2 + pj % 4
            for kc in range(NCD):
                w0 = []
                if kc == 0:
                    w0 = [(dWb[par], ld), (c_ev, H1_EV)] + bw(bk)
                ev("pe", lambda e, par=par, tt=tt, kc=kc, bk=bk: e.matmul(
                    bankf(bk), wblk[:, par, kc, :],
                    h1[:, kc, tt * 512:(tt + 1) * 512],
                    start=(kc == 0), stop=(kc == NCD - 1)),
                   inc=(c_pe if kc == NCD - 1 else None), waits=w0)
            ev("ve", lambda e, kh=kh, tt=tt, bk=bk: e.tensor_copy(
                k_fm[:, kh, 512 + tt * 512:512 + (tt + 1) * 512], bankf(bk)),
               inc=c_ve, waits=[(c_pe, c_pe.n), (c_gp, PADS)])
            brel(bk, c_ve)
            pj += 1
        kh_done[kh] = c_pe.n
    for tc in range(NTC):
        bk = 2 + pj % 4
        for kc in range(NCD):
            w0 = []
            if kc == 0:
                w0 = [(dWv, WV_OK)] + bw(bk)
            ev("pe", lambda e, tc=tc, kc=kc, bk=bk: e.matmul(
                bankf(bk), h1[:, kc, tc * 128:(tc + 1) * 128], wv_rs[:, kc, :],
                start=(kc == 0), stop=(kc == NCD - 1)),
               inc=(c_pe if kc == NCD - 1 else None), waits=w0)
        ev("ve", lambda e, tc=tc, bk=bk: e.tensor_copy(
            v_tm[:, 4 + tc, :], bankf(bk)),
           inc=c_ve, waits=[(c_pe, c_pe.n), (c_gp, PADS)])
        brel(bk, c_ve)
        pj += 1
    VK_EV = c_ve.n
    h_done = {}
    qev_done = {}
    for h in range(H):
        par = h % 2
        wfree = [(c_pe, h_done[h - 2])] if h >= 2 else [
            (c_pe, kh_done[2 + par])]
        ev("gp", lambda e, h=h, par=par: e.dma_start(
            wblk[:, par, :, :],
            wq[:, h * 128:(h + 1) * 128].rearrange("(c p) n -> p c n", p=128)),
           sc=dWb[par], waits=wfree)
        ld = dWb[par].n
        for tt in range(4):
            bk = 2 + pj % 4
            for kc in range(NCD):
                w0 = []
                if kc == 0:
                    w0 = [(dWb[par], ld)] + bw(bk)
                ev("pe", lambda e, par=par, tt=tt, kc=kc, bk=bk: e.matmul(
                    bankf(bk), wblk[:, par, kc, :],
                    h1[:, kc, tt * 512:(tt + 1) * 512],
                    start=(kc == 0), stop=(kc == NCD - 1)),
                   inc=(c_pe if kc == NCD - 1 else None), waits=w0)
            ev("se", lambda e, h=h, tt=tt, bk=bk: e.activation(
                q_fm[:, h, tt * 512:(tt + 1) * 512], bankf(bk), AF.Identity,
                bias=0.0, scale=QSCALE),
               inc=c_ev, waits=[(c_pe, c_pe.n)])
            brel(bk, c_ev)
            pj += 1
        h_done[h] = c_pe.n
        qev_done[h] = c_ev.n
    Q_EV = c_ev.n



    VR0 = c_vr.n
    VD0 = c_vd.n
    NIT = NQ * H

    def it_qc_h(it):
        return it // H, it % H

    def emit_qk(it):
        qc, h = it_qc_h(it)
        vm = min(qc, 4)
        p = it % 2
        qsl = q_fm[:, h, qc * 128:(qc + 1) * 128]
        ksl = k_fm[:, h // 4, qc * 128:qc * 128 + WINC]
        scA = PSf[:, p * 1024:p * 1024 + 512]
        scB = PSf[:, p * 1024 + 512:p * 1024 + WINC]
        w0 = [(c_exp, it - 1), (c_max, it - 1)]
        if it == 0:
            w0 += [(c_ve, VK_EV), (c_ev, Q_EV), (dCn, MASKS_OK),
                   (c_gp, PADS)]
        ev("pe", lambda e, qsl=qsl, ksl=ksl, scA=scA: e.matmul(
            scA, qsl, ksl[:, 0:512], start=True, stop=False), waits=w0)
        ev("pe", lambda e, qsl=qsl, ksl=ksl, scB=scB: e.matmul(
            scB, qsl, ksl[:, 512:WINC], start=True, stop=False))
        ev("pe", lambda e, vm=vm, scA=scA: e.matmul(
            scA, id_bf[:, :], s_masks[:, vm, 0:512], start=False, stop=True))
        ev("pe", lambda e, vm=vm, scB=scB: e.matmul(
            scB, id_bf[:, :], s_masks[:, vm, 512:WINC], start=False,
            stop=True), inc=c_qk)

    def emit_pt(it):
        p = it % 2
        ptb = PSf[:, 2048:2048 + WINC]
        for j in range(5):
            w1 = []
            if j == 0:
                w1 = [(c_exp, it + 1), (c_vd, VD0 + it + 1), (c_pte, it)]
            ev("pe", lambda e, j=j, p=p, ptb=ptb: e.matmul(
                ptb[:, j * 128:(j + 1) * 128],
                s_p[:, p, j * 128:(j + 1) * 128], s_d[:, p, :],
                start=True, stop=True),
               inc=(c_pt if j == 4 else None), waits=w1)

    def emit_av(it):
        qc, h = it_qc_h(it)
        p = it % 2
        avb = PSf[:, 3072 + p * 512:3072 + p * 512 + 128]
        for j in range(5):
            w1 = []
            if j == 0:
                w1 = [(c_pte, it + 1), (c_ave, it - 1)]
            ev("pe", lambda e, j=j, p=p, qc=qc, kh=h // 4, avb=avb: e.matmul(
                avb, v_tm[:, qc + j, kh * 128:(kh + 1) * 128],
                s_pt[:, p, j * 128:(j + 1) * 128],
                start=(j == 0), stop=(j == 4)),
               inc=(c_av if j == 4 else None), waits=w1)

    def emit_max(it):
        p = it % 2
        scAll = PSf[:, p * 1024:p * 1024 + WINC]
        ev("ve", lambda e, scAll=scAll, p=p: e.reduce_max(
            negmax[:, p:p + 1], scAll, AX.X, negate=True),
           inc=c_max,
           waits=[(c_qk, it + 1)] + ([(dCn, ID_OK)] if it == 0 else []))

    def emit_rd(it):
        p = it % 2
        ev("ve", lambda e, p=p: e.reciprocal(rsum[:, p:p + 1],
                                             esum[:, p:p + 1]),
           inc=c_vr, waits=[(c_exp, it + 1)])
        ev("ve", lambda e, p=p: e.tensor_scalar(
            s_d[:, p, :], id_bf[:, :], rsum[:, p:p + 1], None, ALU.mult),
           inc=c_vd, waits=[(c_vr, VR0 + it + 1), (c_pt, it - 1)])

    def emit_ave(it):
        qc, h = it_qc_h(it)
        p = it % 2
        avb = PSf[:, 3072 + p * 512:3072 + p * 512 + 128]
        ev("ve", lambda e, p=p, h=h, qc=qc, avb=avb: e.tensor_copy(
            attn[:, h, qc * 128:(qc + 1) * 128], avb),
           inc=c_ave, waits=[(c_av, it + 1)])

    def emit_exp(it):
        p = it % 2
        scAll = PSf[:, p * 1024:p * 1024 + WINC]
        ev("se", lambda e, scAll=scAll, p=p: e.activation(
            s_p[:, p, :], scAll, AF.Exp, bias=negmax[:, p:p + 1],
            scale=1.0, accum_out=esum[:, p:p + 1]),
           inc=c_exp, waits=[(c_max, it + 1), (c_pt, it - 1)])

    def emit_pte(it):
        p = it % 2
        ptb = PSf[:, 2048:2048 + WINC]
        ev("se", lambda e, p=p, ptb=ptb: e.copy(s_pt[:, p, :], ptb),
           inc=c_pte, waits=[(c_pt, it + 1), (c_av, it - 1)])

    for m in range(NIT + 4):
        if m < NIT:
            emit_qk(m)
        if 2 <= m <= NIT + 1:
            emit_pt(m - 2)
        if 4 <= m <= NIT + 3:
            emit_av(m - 4)
        if 1 <= m <= NIT:
            emit_rd(m - 1)
        if m < NIT:
            emit_max(m)
        if 4 <= m <= NIT + 3:
            emit_ave(m - 4)
        if 2 <= m <= NIT + 1:
            emit_pte(m - 2)
        if m < NIT:
            emit_exp(m)
    ATT_DONE = c_ave.n

    for i in range(6):
        bank_rel[i] = None

    wo_done = {}
    for dc in range(NCD):
        par = dc % 2
        wfree = [(c_pe, wo_done[dc - 2])] if dc >= 2 else [
            (c_pe, h_done[H - 2 + par]), (dCn, WO16_OK)]
        ev("sy", lambda e, dc=dc, par=par: e.dma_start(
            wblk[:, par, :, :],
            wo16[:, dc * 128:(dc + 1) * 128].rearrange(
                "(c p) n -> p c n", p=128)),
           sc=dWo[par], waits=wfree)
        ld = dWo[par].n
        for tt in range(4):
            bk = (dc * 4 + tt) % 4
            for af in range(NCD):
                w0 = []
                if af == 0:
                    w0 = [(dWo[par], ld)] + bw(bk)
                    if dc == 0 and tt == 0:
                        w0 += [(c_ave, ATT_DONE), (c_exp, 256), (c_max, 256)]
                ev("pe", lambda e, par=par, tt=tt, af=af, bk=bk: e.matmul(
                    bankf(bk), wblk[:, par, af, :],
                    attn[:, af, tt * 512:(tt + 1) * 512],
                    start=(af == 0), stop=(af == NCD - 1)),
                   inc=(c_pe if af == NCD - 1 else None), waits=w0)
            ev("se", lambda e, dc=dc, tt=tt, bk=bk: e.activation(
                contrib[:, dc, tt * 512:(tt + 1) * 512], bankf(bk), AF.Identity,
                bias=0.0, scale=modall[:, 32 + dc:33 + dc]),
               inc=c_wo, waits=[(c_pe, c_pe.n), (c_mod, 1)])
            brel(bk, c_wo)
        wo_done[dc] = c_pe.n
    WO_DONE = c_wo.n

    pre_steps = []
    for m in range(NM):
        pre_steps.append((gw[:, m * 128:(m + 1) * 128].rearrange(
            "(c p) n -> p c n", p=128), g_scr[m]))
        pre_steps.append((uw[:, m * 128:(m + 1) * 128].rearrange(
            "(c p) n -> p c n", p=128), u_scr[m]))
    for dc in range(NCD):
        for sub in range(3):
            pre_steps.append((
                dw[2048 * sub:2048 * (sub + 1),
                   dc * 128:(dc + 1) * 128].rearrange(
                    "(c p) n -> p c n", p=128),
                d_scr[dc][:, 2048 * sub:2048 * (sub + 1)]))
    st_done = {}
    ld_done = {}
    NPRE = len(pre_steps)
    for si in range(NPRE + 2):
        if si < NPRE:
            psrc = pre_steps[si][0]
            par = si % 3
            lp = si % 2
            w = [st_done[si - 3]] if si - 3 in st_done else []
            ev("gp", lambda e, psrc=psrc, par=par: e.dma_start(
                pre8[:, par, :].rearrange("p (c n) -> p c n", n=128), psrc),
               sc=dPre2[lp], waits=w)
            ld_done[si] = (dPre2[lp], dPre2[lp].n)
        sj = si - 2
        if sj >= 0:
            dst = pre_steps[sj][1]
            par = sj % 3
            lp = sj % 2
            ev("gp", lambda e, dst=dst, par=par: e.dma_start(
                dst.bitcast(U32), pre8[:, par, :].bitcast(U32)),
               sc=dScr2[lp], waits=[ld_done[sj]])
            st_done[sj] = (dScr2[lp], dScr2[lp].n)
    PRE_A, PRE_B = dScr2[0].n, dScr2[1].n
    pre_st = dict(st_done)



    epi_done = 0
    gup_mm_done = {}
    dn_prev = {}
    for Q in range(4):
        h2tr0 = c_tr.n
        for t4 in range(4):
            tc = Q * 4 + t4
            par = t4 % 2
            sc_ = tc % 16
            rel = [(c_sq, NTC + tc - 1), (c_vxh, NTC + tc - 1)]
            if Q == 0 and t4 < 2:
                rel = [(c_wo, WO_DONE)]
            ev("sy", lambda e, tc=tc, par=par: e.dma_start(
                s_u2[:, par, :], xin[tc * 128:(tc + 1) * 128, :]),
               sc=dX[par], waits=rel)
            nx = dX[par].n
            tr0 = c_tr.n
            for dcg in range(4):
                bk = 4 + dcg
                for d4 in range(4):
                    dc = dcg * 4 + d4
                    w0 = []
                    if d4 == 0:
                        w0 = [(c_wo, WO_DONE), (c_ut, c_ut.n),
                              (c_fve, epi_done)] + bw(4 + dcg)
                    ev("pe", lambda e, dc=dc, d4=d4, tc=tc, dcg=dcg: e.matmul(
                        bankf(4 + dcg)[:, d4 * 128:(d4 + 1) * 128],
                        contrib[:, dc, tc * 128:(tc + 1) * 128], id_bf[:, :],
                        start=(d4 == 0), stop=(d4 == 3)),
                       inc=(c_tr if d4 == 3 else None), waits=w0)
            for dcg in range(4):
                ev("ve", lambda e, dcg=dcg, par=par: e.tensor_tensor(
                    s_u2[:, par, dcg * 512:(dcg + 1) * 512],
                    s_u2[:, par, dcg * 512:(dcg + 1) * 512],
                    bankf(4 + dcg), ALU.add),
                   inc=c_ut, waits=[(c_tr, tr0 + dcg + 1), (dX[par], nx),
                                    (c_vxh, NTC + tc - 1)])
                brel(4 + dcg, c_ut)
            ev("ve", lambda e, sc_=sc_, par=par: e.reduce_sum(
                sumx[:, sc_:sc_ + 1], s_u2[:, par, :], AX.X), sc=c_ve,
               waits=[(c_ut, c_ut.n)])
            ev("se", lambda e, sc_=sc_, par=par: e.activation(
                sq2[:, :], s_u2[:, par, :], AF.Square, bias=0.0, scale=1.0,
                accum_out=ssq[:, sc_:sc_ + 1]),
               sc=c_sq, waits=[(c_ut, c_ut.n)])
            ev("ve", lambda e, sc_=sc_: e.tensor_scalar(
                smean[:, sc_:sc_ + 1], sumx[:, sc_:sc_ + 1], INV_D, None,
                ALU.mult), waits=[(c_sq, c_sq.n)], sc=c_ve)
            ev("ve", lambda e, sc_=sc_: e.tensor_scalar(
                svar[:, sc_:sc_ + 1], ssq[:, sc_:sc_ + 1], INV_D, EPS,
                ALU.mult, ALU.add), sc=c_ve)
            ev("ve", lambda e, sc_=sc_: e.tensor_tensor(
                tmpa[:, sc_:sc_ + 1], smean[:, sc_:sc_ + 1],
                smean[:, sc_:sc_ + 1], ALU.mult), sc=c_ve)
            ev("ve", lambda e, sc_=sc_: e.tensor_tensor(
                svar[:, sc_:sc_ + 1], svar[:, sc_:sc_ + 1],
                tmpa[:, sc_:sc_ + 1], ALU.subtract), sc=c_ve, inc=c_vst)
            ev("se", lambda e, sc_=sc_: e.sqrt(
                ssd[:, sc_:sc_ + 1], svar[:, sc_:sc_ + 1]),
               inc=c_ssd, waits=[(c_vst, c_vst.n)])
            ev("ve", lambda e, sc_=sc_: e.reciprocal(
                srstd[:, sc_:sc_ + 1], ssd[:, sc_:sc_ + 1]),
               inc=c_vr, waits=[(c_ssd, c_ssd.n)])
            ev("ve", lambda e, sc_=sc_, par=par, t4=t4: e.tensor_scalar(
                s_xh2[:, t4, :], s_u2[:, par, :], smean[:, sc_:sc_ + 1],
                srstd[:, sc_:sc_ + 1], ALU.subtract, ALU.mult),
               inc=c_vxh, waits=[(c_vr, c_vr.n), (c_tr, h2tr0)])
        XH4 = c_vxh.n
        for dc in range(NCD):
            bk = dc % 2
            for t4 in range(4):
                w0 = []
                if t4 == 0:
                    w0 = [(c_vxh, XH4)] + bw(6 + bk)
                ev("pe", lambda e, t4=t4, dc=dc, bk=bk: e.matmul(
                    bankf(6 + bk)[:, t4 * 128:(t4 + 1) * 128],
                    s_xh2[:, t4, dc * 128:(dc + 1) * 128], id_bf[:, :],
                    start=(t4 == 0), stop=(t4 == 3)),
                   inc=(c_tr if t4 == 3 else None), waits=w0)
            ev("se", lambda e, dc=dc, bk=bk: e.activation(
                h2[:, dc, :], bankf(6 + bk), AF.Identity,
                bias=modall[:, 48 + dc:49 + dc],
                scale=modall[:, 64 + dc:65 + dc]),
               inc=c_h2, waits=[(c_tr, c_tr.n)])
            brel(6 + bk, c_h2)
        H2Q = c_h2.n
        stt_done = {}
        for m in range(NM):
            par = m % 2
            gidx = Q * NM + m
            w0 = [pre_st[2 * m]]
            if Q == 0 and m < 2:
                w0 += [(c_wo, WO_DONE)]
            if gidx >= 2:
                w0 += [(c_fpe, gup_mm_done[gidx - 2])]
            ev("sy", lambda e, m=m, par=par: e.dma_start(g8[:, par, :],
                                                         g_scr[m]),
               sc=dG[par], waits=w0)
            ldg = dG[par].n
            ev("sy", lambda e, m=m, par=par: e.dma_start(u8w[:, par, :],
                                                         u_scr[m]),
               sc=dU[par], waits=[pre_st[2 * m + 1]])
            ldu = dU[par].n
            for kc in range(NCD):
                w1 = []
                if kc == 0:
                    w1 = [(dG[par], ldg), (c_h2, H2Q)] + bw(0 + par)
                ev("pe", lambda e, par=par, kc=kc: e.matmul(
                    bankf(0 + par),
                    g8[:, par, :].rearrange("p (c n) -> p c n", c=16)[
                        :, kc, :].bitcast(F8),
                    h2[:, kc, :], start=(kc == 0), stop=(kc == NCD - 1)),
                   waits=w1)
            for kc in range(NCD):
                w1 = []
                if kc == 0:
                    w1 = [(dU[par], ldu)] + bw(2 + par)
                ev("pe", lambda e, par=par, kc=kc: e.matmul(
                    bankf(2 + par),
                    u8w[:, par, :].rearrange("p (c n) -> p c n", c=16)[
                        :, kc, :].bitcast(F8),
                    h2[:, kc, :], start=(kc == 0), stop=(kc == NCD - 1)),
                   inc=(c_fpe if kc == NCD - 1 else None), waits=w1)
            gup_mm_done[gidx] = c_fpe.n
            w2 = [(c_fpe, c_fpe.n), (c_mod, 1)]
            if m >= 2:
                w2 += [(c_fve, stt_done[m - 2])]
            elif Q > 0:
                w2 += [(c_fve, epi_done)]
            ev("se", lambda e, m=m, par=par: e.activation(
                sy2[:, par, :], bankf(0 + par), AF.Silu, bias=0.0,
                scale=gsus[:, m:m + 1]),
               inc=c_fse, waits=w2)
            brel(0 + par, c_fse)
            ev("ve", lambda e, m=m, par=par: e.scalar_tensor_tensor(
                fuse[:, m, :], bankf(2 + par), gsus[:, 48 + m:49 + m],
                sy2[:, par, :], ALU.mult, ALU.mult),
               inc=c_fve,
               waits=[(c_fse, c_fse.n)]
               + ([(c_ev, 4)] if (Q == 0 and m == 0) else []))
            brel(2 + par, c_fve)
            stt_done[m] = c_fve.n
        FUSEQ = c_fve.n
        GUP_EV = c_fse.n
        dn_mm_done = {}
        yev_done = {}
        stt_y = {}
        trg_after = {}

        def emit_downmm(dc):
            par = dc % 2
            didx = Q * NCD + dc
            w0 = [pre_st[96 + 3 * dc + 1], pre_st[96 + 3 * dc + 2]]
            if Q == 0 and dc < 2:
                w0 += [(c_av, 256), (c_pt, 256), (c_qk, 256)]
            elif dc < 2:
                w0 += [(c_fpe, dn_prev[14 + dc])]
            else:
                w0 += [(c_fpe, dn_mm_done[dc - 2])]
            ev("sy", lambda e, dc=dc, par=par: e.dma_start(
                d8[:, par, :], d_scr[dc]),
               sc=dD8[par], waits=w0)
            ldd = dD8[par].n
            dbk = dc % 2
            for m_ in range(NM):
                w1 = []
                if m_ == 0:
                    w1 = [(dD8[par], ldd), (c_fve, FUSEQ)] + bw(dbk)
                ev("pe", lambda e, par=par, m=m_, dbk=dbk: e.matmul(
                    bankf(dbk),
                    d8[:, par, :].rearrange("p (a n) -> p a n", a=NM)[
                        :, m, :].bitcast(F8),
                    fuse[:, m, :], start=(m == 0), stop=(m == NM - 1)),
                   inc=(c_fpe if m_ == NM - 1 else None), waits=w1)
            dn_mm_done[dc] = c_fpe.n
            w2 = [(c_fpe, c_fpe.n)]
            if Q == 0 and dc == 0:
                w2 += [(c_mod, 1)]
            if dc >= 2:
                w2 += [(c_tr, yev_done[dc - 2])]
            ev("ve", lambda e, par=par, dc=dc, dbk=dbk, Q=Q:
               e.scalar_tensor_tensor(
                   sy2[:, par, :], bankf(dbk), gbds[:, dc:dc + 1],
                   contrib[:, dc, Q * 512:(Q + 1) * 512],
                   ALU.mult, ALU.add),
               inc=c_fve, waits=w2)
            brel(dbk, c_fve)
            stt_y[dc] = c_fve.n

        def emit_transp(dc):
            par = dc % 2
            d4 = dc % 4
            for t4 in range(4):
                w3 = []
                if t4 == 0:
                    w3 = [(c_fve, stt_y[dc])]
                    if d4 == 0:
                        w3 += bw(4) + bw(5) + bw(6) + bw(7)
                ev("pe", lambda e, t4=t4, par=par, d4=d4: e.matmul(
                    bankf(4 + t4)[:, d4 * 128:(d4 + 1) * 128],
                    sy2[:, par, t4 * 128:(t4 + 1) * 128], id_bf[:, :],
                    start=True, stop=True),
                   inc=(c_tr if t4 == 3 else None), waits=w3)
            yev_done[dc] = c_tr.n

        def emit_xe(dcg):
            guard = FUSEQ if dcg < 2 else epi_add_after[dcg - 2]
            for t4 in range(4):
                tc = Q * 4 + t4
                ev("gp", lambda e, tc=tc, t4=t4, dcg=dcg: e.dma_start(
                    s_xe2[:, t4, dcg % 2, :],
                    xin[tc * 128:(tc + 1) * 128, dcg * 512:(dcg + 1) * 512]),
                   sc=dXE, waits=([(c_fve, guard)] if t4 == 0 else []))
            xe_cnt[dcg] = dXE.n

        def emit_epilogue(dcg):
            TRG = trg_after[dcg]
            for t4 in range(4):
                tc = Q * 4 + t4
                g = Q * 4 + dcg
                ev("ve", lambda e, t4=t4, dcg=dcg: e.tensor_tensor(
                    x3p[:, t4, :], s_xe2[:, t4, dcg % 2, :],
                    bankf(4 + t4), ALU.add),
                   inc=c_fve,
                   waits=[(c_tr, TRG), (dXE, xe_cnt[dcg]),
                          (dOL[t4], 16 * g)])
                brel(4 + t4, c_fve)
                ev("gp", lambda e, tc=tc, dcg=dcg, t4=t4: e.dma_start(
                    yout[tc * 128:(tc + 1) * 128,
                         dcg * 512:(dcg + 1) * 512], x3p[:, t4, :]),
                   sc=dOL[t4], waits=[(c_fve, c_fve.n)])
            epi_add_after[dcg] = c_fve.n

        xe_cnt = {}
        epi_add_after = {}
        for dc in range(NCD + 1):
            if dc < NCD:
                if dc % 4 == 0:
                    emit_xe(dc // 4)
                emit_downmm(dc)
            if dc >= 1:
                emit_transp(dc - 1)
                if (dc - 1) % 4 == 3:
                    trg_after[(dc - 1) // 4] = c_tr.n
                    emit_epilogue((dc - 1) // 4)
                dn_prev = dn_mm_done
        epi_done = c_fve.n

    drain = [(s, s.n) for s in dOL] + [(dO, dO.n)]
    ev("sy", lambda e: e.nop(), waits=drain)
    ev("gp", lambda e: e.nop(), waits=drain)

    for name in DBG:
        tap = {
            "h1": (h1, c_ev, H1_EV),
            "q": (q_fm, c_ve, QKV_EV),
            "attn": (attn, c_ave, ATT_DONE),
            "contrib": (contrib, c_wo, WO_DONE),
            "modall": (modall, c_mod, 1),
            "kf": (k_fm, c_ve, QKV_EV),
            "vt": (v_tm, c_ve, QKV_EV),
        }[name]
        tsr, ctr, cnt = tap
        flat = tsr.ap()
        if len(flat.shape) == 3:
            flat = flat.rearrange("p a b -> p (a b)")
        out_t = nc.dram_tensor("dbg_" + name, list(flat.shape), F32,
                               kind="ExternalOutput").ap()
        ev("gp", lambda e, flat=flat, out_t=out_t: e.dma_start(out_t, flat),
           sc=dO, waits=[(ctr, cnt)])
        ev("gp", lambda e, n=dO.n: e.nop(), waits=[(dO, dO.n)])

    _assemble()
    stack.close()
    return nc


def _host_consts():
    i = np.arange(128)[:, None]
    j = np.arange(WINC)[None, :]
    band = (j >= i + 1) & (j <= i + 512)
    masks = np.empty((5, 128, WINC), np.float32)
    for v in range(5):
        cutoff = 512 - v * 128 if v < 4 else 0
        allowed = band & (j >= cutoff)
        masks[v] = np.where(allowed, 0.0, -30000.0).astype(np.float32)
    return masks, np.eye(128, dtype=np.float32)


def _pad_vec(v, rows):
    out = np.zeros((128, 128), np.float32)
    out[:rows] = np.asarray(v, np.float32).reshape(rows, 128)
    return out


def kernel(**inputs):
    f32 = lambda a: np.ascontiguousarray(np.asarray(a, np.float32))
    x = f32(inputs["L0_attn_x"])
    masks, ident = _host_consts()
    common = {
        "wq": f32(inputs["L0_attn_Wq"]),
        "wk": f32(inputs["L0_attn_Wk"]),
        "wv": f32(inputs["L0_attn_Wv"]),
        "wo": f32(inputs["L0_attn_Wo"]),
        "wmod": f32(inputs["L0_wmod"]),
        "gw": f32(inputs["L0_ffn_gate_w"]),
        "uw": f32(inputs["L0_ffn_up_w"]),
        "dw": f32(inputs["L0_ffn_down_w"]),
        "gs_p": _pad_vec(inputs["L0_ffn_gate_scale"], 48),
        "us_p": _pad_vec(inputs["L0_ffn_up_scale"], 48),
        "ds_p": _pad_vec(inputs["L0_ffn_down_scale"], 16),
        "ident": ident,
        "masks": masks,
    }
    c_all = f32(inputs["L0_c"])
    if "nc" not in _CACHE:
        _CACHE["nc"] = build()
    nc = _CACHE["nc"]
    in_maps = []
    for b in range(8):
        m = dict(common)
        m["x"] = np.ascontiguousarray(x[b])
        m["c_p"] = _pad_vec(c_all[b], 16)
        in_maps.append(m)
    res = run_bass_kernel_spmd(nc, in_maps, core_ids=list(range(8)))
    return np.stack([res.results[b]["y"] for b in range(8)], axis=0)


# version 20 (final)
# speedup vs baseline: 1.1025x; 1.0134x over previous
import math
from contextlib import ExitStack

import numpy as np

import concourse.bass as bass
import concourse.mybir as mybir
from concourse.bass_utils import run_bass_kernel_spmd

F32 = mybir.dt.float32
BF16 = mybir.dt.bfloat16
F8 = mybir.dt.float8e4
U8 = mybir.dt.uint8
U32 = mybir.dt.uint32
AF = mybir.ActivationFunctionType
ALU = mybir.AluOpType
AX = mybir.AxisListType

D = 2048
T = 2048
NCD = 16
NTC = 16
H, HK, HD = 16, 4, 128
NQ = 16
WINC = 640
I = 6144
NM = 48
EPS = 1e-5
QSCALE = 1.0 / math.sqrt(HD)
INV_D = 1.0 / D

DBG = []
_CACHE = {}


class Ctr:
    def __init__(self, stack, nc, name, step=1):
        self.h = stack.enter_context(nc.semaphore(name))
        self.step = step
        self.n = 0

    def bump(self):
        self.n += self.step
        return self.n


def build():
    nc = bass.Bass("TRN2", target_bir_lowering=False)
    stack = ExitStack()

    xin = nc.dram_tensor("x", [T, D], F32, kind="ExternalInput").ap()
    wq = nc.dram_tensor("wq", [D, H * HD], F32, kind="ExternalInput").ap()
    wk = nc.dram_tensor("wk", [D, HK * HD], F32, kind="ExternalInput").ap()
    wv = nc.dram_tensor("wv", [D, HK * HD], F32, kind="ExternalInput").ap()
    wo = nc.dram_tensor("wo", [H * HD, D], F32, kind="ExternalInput").ap()
    wmod = nc.dram_tensor("wmod", [D, 6 * D], F32, kind="ExternalInput").ap()
    gw = nc.dram_tensor("gw", [D, I], F32, kind="ExternalInput").ap()
    uw = nc.dram_tensor("uw", [D, I], F32, kind="ExternalInput").ap()
    dw = nc.dram_tensor("dw", [I, D], F32, kind="ExternalInput").ap()
    c_p = nc.dram_tensor("c_p", [128, 128], F32, kind="ExternalInput").ap()
    gs_p = nc.dram_tensor("gs_p", [128, 128], F32, kind="ExternalInput").ap()
    us_p = nc.dram_tensor("us_p", [128, 128], F32, kind="ExternalInput").ap()
    ds_p = nc.dram_tensor("ds_p", [128, 128], F32, kind="ExternalInput").ap()
    ident = nc.dram_tensor("ident", [128, 128], F32, kind="ExternalInput").ap()
    masks = nc.dram_tensor("masks", [5, 128, WINC], F32, kind="ExternalInput").ap()
    yout = nc.dram_tensor("y", [T, D], F32, kind="ExternalOutput").ap()

    mod_scr = nc.dram_tensor("mod_scr", [1, 6 * D], F32, kind="Internal").ap()
    g_scr = nc.dram_tensor("g_scr", [NM, 128, 2048], U8, kind="Internal").ap()
    u_scr = nc.dram_tensor("u_scr", [NM, 128, 2048], U8, kind="Internal").ap()
    d_scr = nc.dram_tensor("d_scr", [NCD, 128, 6144], U8, kind="Internal").ap()
    wo16 = nc.dram_tensor("wo16", [H * HD, D], BF16, kind="Internal").ap()

    BASE = 16640
    LIMIT = 229200
    cur = [BASE]

    OFFS = {}

    def sb(name, shape, dtype, off=None, align=64):
        bpp = int(np.prod(shape[1:])) * mybir.dt.size(dtype)
        if off is None:
            cur[0] = (cur[0] + align - 1) // align * align
            off = cur[0]
            cur[0] += bpp
        assert off + bpp <= LIMIT, (name, off, bpp)
        OFFS[name] = off
        return nc.alloc_sbuf_tensor_at(name, shape, dtype, offset=off)

    id_bf = sb("id_bf", [128, 128], BF16)
    id32 = sb("id32", [128, 128], F32)
    modall = sb("modall", [128, 96], F32)
    gsus = sb("gsus", [128, 96], F32)
    dsq = sb("dsq", [128, 16], F32)
    gbds = sb("gbds", [128, 16], F32)
    csi = sb("csi", [128, 16], BF16)
    tmpa = sb("tmpa", [128, 16], F32)
    tmpb = sb("tmpb", [128, 16], F32)
    tmpc = sb("tmpc", [128, 16], F32)
    sumx = sb("sumx", [128, 16], F32)
    ssq = sb("ssq", [128, 16], F32)
    smean = sb("smean", [128, 16], F32)
    svar = sb("svar", [128, 16], F32)
    ssd = sb("ssd", [128, 16], F32)
    srstd = sb("srstd", [128, 16], F32)
    negmax = sb("negmax", [128, 2], F32)
    esum = sb("esum", [128, 2], F32)
    rsum = sb("rsum", [128, 2], F32)
    vecs = sb("vecs", [128, 128], F32)
    x3p = sb("x3p", [128, 4, 512], F32)
    sy2 = sb("sy2", [128, 2, 512], BF16)
    PERS_END = cur[0]

    O_ATN = (PERS_END + 63) // 64 * 64
    s_masks = sb("s_masks", [128, 5, WINC], BF16, off=O_ATN)
    s_p = sb("s_p", [128, 3, WINC], BF16, off=O_ATN + 6400)
    s_pt = sb("s_pt", [128, 2, WINC], BF16, off=O_ATN + 6400 + 3840)
    s_d = sb("s_d", [128, 2, 128], BF16, off=O_ATN + 6400 + 6400)
    d8 = sb("d8", [128, 2, 6144], U8, off=O_ATN)
    O_WBUF = O_ATN + 13376

    wblk = sb("wblk", [128, 2, 16, 128], BF16, off=O_WBUF)
    g8 = sb("g8", [128, 2, 2048], U8, off=O_WBUF)
    u8w = sb("u8w", [128, 2, 2048], U8, off=O_WBUF + 4096)
    O_KV = O_WBUF + 8192

    k_fm = sb("k_fm", [128, HK, T + 512], BF16, off=O_KV)
    v_tm = sb("v_tm", [128, NTC + 4, 512], BF16, off=O_KV + 20480)
    s_u2 = sb("s_u2", [128, 2, D], F32, off=O_KV)
    s_xh2 = sb("s_xh2", [128, 4, D], BF16, off=O_KV + 16384)
    sq2 = sb("sq2", [128, D], F32, off=O_KV + 32768)
    xhq2 = sb("xhq2", [128, 4, D], BF16, off=O_KV + 16384)
    xhq3 = sb("xhq3", [128, 4, D], BF16, off=O_KV)
    O_A = O_KV + 41088

    wmodb = sb("wmodb", [128, 6, 3072], BF16, off=O_A)
    h1 = sb("h1", [128, NCD, T], BF16, off=O_A)
    attn = sb("attn", [128, H, T], BF16, off=O_A)
    fuse = sb("fuse", [128, NM, 512], BF16, off=O_A)
    h2 = sb("h2", [128, NCD, 512], BF16, off=O_A + 49152)
    s_xe2 = sb("s_xe2", [128, 4, 2, 512], F32, off=O_A + 49152)
    O_B = O_A + 65536

    s_x1 = sb("s_x1", [128, 2, D], F32, off=O_B)
    xhq = sb("xhq", [128, 4, D], BF16, off=O_B + 16384)
    sq1 = sb("sq1", [128, D], F32, off=O_B + 32768)
    modtmp = sb("modtmp", [1, 3072], F32, off=O_B + 40960)
    wv_rs = sb("wv_rs", [128, 16, 512], BF16, off=O_B + 16384)
    q_fm = sb("q_fm", [128, H, T], BF16, off=O_B)
    contrib = sb("contrib", [128, NCD, T], BF16, off=O_B)
    O_TOP = O_B + 65536
    pre8 = sb("pre8", [128, 2, 2048], F8, off=O_TOP)
    assert O_TOP + 4096 <= LIMIT

    PSf = nc.alloc_psum_tensor("PSf", [128, 4096], F32)

    def bankf(i, w=512):
        return PSf[:, 512 * i:512 * i + w]

    OPS = {k: [] for k in ("pe", "ve", "se", "gp", "sy")}

    def C(name, step=1):
        return Ctr(stack, nc, name, step)

    dCn = C("dCn", 16)
    dWm = [C("dWm%d" % i, 16) for i in range(6)]
    dWb = [C("dWb0", 16), C("dWb1", 16)]
    dPre = C("dPre", 16)
    dScr = C("dScr", 16)
    dMT = C("dMT", 16)
    dX = [C("dX0", 16), C("dX1", 16)]
    dXE = C("dXE", 16)
    dG = [C("dG0", 16), C("dG1", 16)]
    dU = [C("dU0", 16), C("dU1", 16)]
    dD8 = [C("dD80", 16), C("dD81", 16)]
    dWo = [C("dWo0", 16), C("dWo1", 16)]
    dPre2 = [C("dPreA", 16), C("dPreB", 16)]
    dScr2 = [C("dScrA", 16), C("dScrB", 16)]
    dO = C("dO", 16)
    dOL = [C("dO0", 16), C("dO1", 16), C("dO2", 16), C("dO3", 16)]
    dWv = C("dWv", 16)

    c_pe = C("c_pe")
    c_ve = C("c_ve")
    c_se = C("c_se")
    c_gp = C("c_gp")
    c_mod = C("c_mod")
    c_sgn = C("c_sgn")
    c_sq = C("c_sq")
    c_ssd = C("c_ssd")
    c_vst = C("c_vst")
    c_vr = C("c_vr")
    c_vxh = C("c_vxh")
    c_tr = C("c_tr")
    c_ev = C("c_ev")
    c_lau = C("c_lau")
    c_qk = C("c_qk")
    c_max = C("c_max")
    c_exp = C("c_exp")
    c_vd = C("c_vd")
    c_pt = C("c_pt")
    c_pte = C("c_pte")
    c_vpte = C("c_vpte")
    c_av = C("c_av")
    c_ave = C("c_ave")
    c_wo = C("c_wo")
    c_fpe = C("c_fpe")
    c_fse = C("c_fse")
    c_fve = C("c_fve")
    c_h2 = C("c_h2")
    c_ut = C("c_ut")

    def ev(eng, fn, inc=None, waits=(), sc=None):
        waits = list(waits)
        if sc is not None:
            waits.append((sc, sc.n))
            if inc is None:
                inc = sc
        w = tuple((c.h, v) for (c, v) in waits if v > 0)
        tgt = None
        if inc is not None:
            inc.bump()
            tgt = (inc.h, inc.step)

        def run(e, fn=fn, w=w, tgt=tgt):
            for hh, vv in w:
                e.wait_ge(hh, vv)
            r = fn(e)
            if tgt is not None:
                r.then_inc(tgt[0], tgt[1])
        OPS[eng].append(run)

    bank_rel = {i: None for i in range(8)}

    def bw(i):
        r = bank_rel[i]
        return [] if r is None else [(r[0], r[1])]

    def brel(i, ctr):
        bank_rel[i] = (ctr, ctr.n)

    def _assemble():
        with nc.Block() as block:
            @block.sync
            def _(e):
                for f in OPS["sy"]:
                    f(e)

            @block.gpsimd
            def _(e):
                for f in OPS["gp"]:
                    f(e)

            @block.tensor
            def _(e):
                for f in OPS["pe"]:
                    f(e)

            @block.vector
            def _(e):
                for f in OPS["ve"]:
                    f(e)

            @block.scalar
            def _(e):
                for f in OPS["se"]:
                    f(e)

    ev("gp", lambda e: e.dma_start(id_bf[:, :], ident), sc=dCn)
    ev("gp", lambda e: e.dma_start(id32[:, :], ident), sc=dCn)
    ID_OK = dCn.n
    for vi, src in enumerate((c_p, gs_p, us_p, ds_p)):
        ev("gp", lambda e, s=src: e.dma_start(vecs[:, :], s),
           sc=dCn, waits=[(c_pe, vi)])
        ev("pe", lambda e: e.matmul(bankf(5, 128), vecs[:, :], id32[:, :],
                                    start=True, stop=True),
           inc=c_pe, waits=[(dCn, dCn.n)] + bw(5))
        if vi == 0:
            ev("se", lambda e: e.activation(csi[:, :], bankf(5, 16), AF.Silu,
                                            bias=0.0, scale=1.0),
               inc=c_ev, waits=[(c_pe, c_pe.n)])
            brel(5, c_ev)
        elif vi == 1:
            ev("se", lambda e: e.copy(gsus[:, 0:48], bankf(5, 48)),
               inc=c_ev, waits=[(c_pe, c_pe.n)])
            brel(5, c_ev)
        elif vi == 2:
            ev("se", lambda e: e.copy(gsus[:, 48:96], bankf(5, 48)),
               inc=c_ev, waits=[(c_pe, c_pe.n)])
            brel(5, c_ev)
        else:
            ev("se", lambda e: e.copy(dsq[:, :], bankf(5, 16)),
               inc=c_ev, waits=[(c_pe, c_pe.n)])
            brel(5, c_ev)
    VEC_EV = c_ev.n

    XHB = [xhq, xhq2, xhq3, xhq2]
    ln_tr_done = [0] * 9
    xld = {}
    xh_done = {}

    def emit_xload(tc):
        par = tc % 2
        rel = [(c_sq, tc - 1), (c_vxh, tc - 1)]
        ev("sy", lambda e, tc=tc, par=par: e.dma_start(
            s_x1[:, par, :], xin[tc * 128:(tc + 1) * 128, :]),
           sc=dX[par], waits=rel)
        xld[tc] = (dX[par], dX[par].n)

    def emit_stats(tc):
        par = tc % 2
        nx = xld[tc]
        ev("ve", lambda e, tc=tc, par=par: e.reduce_sum(
            sumx[:, tc:tc + 1], s_x1[:, par, :], AX.X),
           waits=[nx], sc=c_ve)
        ev("se", lambda e, tc=tc, par=par: e.activation(
            sq1[:, :], s_x1[:, par, :], AF.Square, bias=0.0, scale=1.0,
            accum_out=ssq[:, tc:tc + 1]),
           sc=c_sq, waits=[nx])
        ev("ve", lambda e, tc=tc: e.tensor_scalar(
            smean[:, tc:tc + 1], sumx[:, tc:tc + 1], INV_D, None, ALU.mult),
           waits=[(c_sq, tc + 1)], sc=c_ve)
        ev("ve", lambda e, tc=tc: e.tensor_scalar(
            svar[:, tc:tc + 1], ssq[:, tc:tc + 1], INV_D, EPS, ALU.mult,
            ALU.add), sc=c_ve)
        ev("ve", lambda e, tc=tc: e.tensor_tensor(
            tmpa[:, tc:tc + 1], smean[:, tc:tc + 1], smean[:, tc:tc + 1],
            ALU.mult), sc=c_ve)
        ev("ve", lambda e, tc=tc: e.tensor_tensor(
            svar[:, tc:tc + 1], svar[:, tc:tc + 1], tmpa[:, tc:tc + 1],
            ALU.subtract), sc=c_ve, inc=c_vst)
        ev("se", lambda e, tc=tc: e.sqrt(ssd[:, tc:tc + 1], svar[:, tc:tc + 1]),
           inc=c_ssd, waits=[(c_vst, c_vst.n)])
        ev("ve", lambda e, tc=tc: e.reciprocal(srstd[:, tc:tc + 1],
                                               ssd[:, tc:tc + 1]),
           inc=c_vr, waits=[(c_ssd, c_ssd.n)])

    def emit_xhat(tc, extra=()):
        par = tc % 2
        q4 = tc % 4
        qq = tc // 4
        buf = XHB[qq]
        ev("ve", lambda e, tc=tc, par=par, q4=q4, buf=buf: e.tensor_scalar(
            buf[:, q4, :], s_x1[:, par, :], smean[:, tc:tc + 1],
            srstd[:, tc:tc + 1], ALU.subtract, ALU.mult),
           inc=c_vxh, waits=[(c_vr, c_vr.n)] + list(extra))
        if q4 == 3:
            xh_done[qq] = c_vxh.n

    for tc in range(14):
        emit_xload(tc)
        emit_stats(tc)
        if tc < 12:
            emit_xhat(tc)

    wk_ld = {}
    for kh in (0, 1):
        ev("gp", lambda e, kh=kh: e.dma_start(
            wblk[:, kh, :, :],
            wk[:, kh * 128:(kh + 1) * 128].rearrange("(c p) n -> p c n",
                                                     p=128)),
           sc=dWb[kh])
        wk_ld[kh] = dWb[kh].n

    chunk_done = {}
    for g in range(4):
        for kc in range(NCD):
            par = (g * NCD + kc) % 6
            idx = g * NCD + kc
            wfree = [(c_pe, chunk_done[idx - 6])] if idx >= 6 else []
            ev("gp", lambda e, g=g, kc=kc, par=par: e.dma_start(
                wmodb[:, par, :],
                wmod[kc * 128:(kc + 1) * 128, g * 3072:(g + 1) * 3072]),
               sc=dWm[par], waits=wfree)
            ld = dWm[par].n
            for nt in range(6):
                w0 = []
                if nt == 0:
                    w0 = [(dWm[par], ld), (c_ev, VEC_EV)]
                    if kc == 0:
                        w0 += bw(0) + bw(1) + bw(2) + bw(3) + bw(4) + bw(5)
                ev("pe", lambda e, kc=kc, par=par, nt=nt: e.matmul(
                    PSf[0:1, nt * 512:(nt + 1) * 512], csi[:, kc:kc + 1],
                    wmodb[:, par, nt * 512:(nt + 1) * 512],
                    start=(kc == 0), stop=(kc == NCD - 1)),
                   inc=(c_pe if nt == 5 else None), waits=w0)
            chunk_done[idx] = c_pe.n
        ev("ve", lambda e: e.tensor_copy(modtmp[:, :], PSf[0:1, 0:3072]),
           inc=c_ve, waits=[(c_pe, c_pe.n), (dMT, dMT.n)])
        ev("gp", lambda e, g=g: e.dma_start(
            mod_scr[:, g * 3072:(g + 1) * 3072], modtmp[:, :]),
           sc=dMT, waits=[(c_ve, c_ve.n)])
        for i in range(6):
            brel(i, c_ve)
    ev("gp", lambda e: e.dma_start(
        vecs[0:96, :], mod_scr.rearrange("a (r n) -> (a r) n", n=128)),
       sc=dMT, waits=[(c_pe, c_pe.n)])
    ev("pe", lambda e: e.matmul(bankf(5, 128), vecs[:, :], id32[:, :],
                                start=True, stop=True),
       inc=c_pe, waits=[(dMT, dMT.n)] + bw(5))
    ev("ve", lambda e: e.tensor_copy(modall[:, :], bankf(5, 96)),
       inc=c_ve, waits=[(c_pe, c_pe.n)])
    brel(5, c_ve)
    MODCP = c_ve.n
    ev("ve", lambda e: e.tensor_scalar(modall[:, 16:32], modall[:, 16:32],
                                       1.0, None, ALU.add), sc=c_ve)
    ev("ve", lambda e: e.tensor_scalar(modall[:, 64:80], modall[:, 64:80],
                                       1.0, None, ALU.add), sc=c_ve)
    for b_ in (32, 80):
        ev("se", lambda e, b=b_: e.activation(
            tmpb[:, :], modall[:, b:b + 16], AF.Sign, bias=0.0, scale=1.0),
           inc=c_sgn, waits=[(c_ve, c_ve.n)])
        ev("ve", lambda e: e.tensor_tensor(tmpc[:, :], tmpb[:, :], tmpb[:, :],
                                           ALU.mult), sc=c_ve,
           waits=[(c_sgn, c_sgn.n)])
        ev("ve", lambda e: e.scalar_tensor_tensor(
            tmpb[:, :], tmpb[:, :], 1.0, tmpc[:, :], ALU.add, ALU.subtract),
           sc=c_ve)
        ev("ve", lambda e, b=b_: e.tensor_tensor(
            tmpa[:, :], modall[:, b:b + 16], tmpb[:, :], ALU.mult), sc=c_ve)
        ev("ve", lambda e: e.tensor_scalar(
            tmpa[:, :], tmpa[:, :], 0.25, None, ALU.max), sc=c_ve)
        ev("ve", lambda e, b=b_: e.tensor_tensor(
            modall[:, b:b + 16], tmpa[:, :], tmpb[:, :], ALU.mult), sc=c_ve)
    ev("ve", lambda e: e.tensor_tensor(gbds[:, :], modall[:, 80:96], dsq[:, :],
                                       ALU.mult), sc=c_ve, inc=c_mod)

    ev("gp", lambda e: e.dma_start(s_masks[:, :, :],
                                   masks.rearrange("v p n -> p v n")),
       sc=dCn)
    MASKS_OK = dCn.n

    def emit_trq(qq):
        buf = XHB[qq]
        for dc in range(NCD):
            bk = dc % 2
            for t4 in range(4):
                w0 = []
                if t4 == 0:
                    w0 = [(c_vxh, xh_done[qq]), (dCn, ID_OK)] + bw(6 + bk)
                ev("pe", lambda e, t4=t4, dc=dc, bk=bk, buf=buf: e.matmul(
                    bankf(6 + bk)[:, t4 * 128:(t4 + 1) * 128],
                    buf[:, t4, dc * 128:(dc + 1) * 128], id_bf[:, :],
                    start=(t4 == 0), stop=(t4 == 3)),
                   inc=(c_tr if t4 == 3 else None), waits=w0)
            ev("se", lambda e, dc=dc, bk=bk, qq=qq: e.activation(
                h1[:, dc, qq * 512:(qq + 1) * 512], bankf(6 + bk),
                AF.Identity, bias=modall[:, dc:dc + 1],
                scale=modall[:, 16 + dc:17 + dc]),
               inc=c_ev, waits=[(c_tr, c_tr.n), (c_mod, 1)])
            brel(6 + bk, c_ev)
        ln_tr_done[qq + 1] = c_tr.n
        h1_ev_q[qq] = c_ev.n

    h1_ev_q = {}
    emit_trq(0)
    emit_trq(1)
    TRQ1 = ln_tr_done[2]
    emit_xhat(12, extra=[(c_tr, TRQ1)])
    emit_xhat(13, extra=[(c_tr, TRQ1)])
    emit_xload(14)
    emit_stats(14)
    emit_xhat(14, extra=[(c_tr, TRQ1)])
    emit_xload(15)
    emit_stats(15)
    emit_xhat(15, extra=[(c_tr, TRQ1)])
    emit_trq(2)
    emit_trq(3)
    H1_EV = c_ev.n
    LN1_TR = c_tr.n

    ev("gp", lambda e: e.memset(k_fm[:, :, 0:512], 0.0), inc=c_gp,
       waits=[(c_tr, LN1_TR)])
    ev("gp", lambda e: e.memset(v_tm[:, 0:4, :], 0.0), inc=c_gp)
    PADS = c_gp.n

    ev("gp", lambda e: e.dma_start(
        wv_rs[:, :, :], wv.rearrange("(c p) n -> p c n", p=128)),
       sc=dWv, waits=[(c_tr, ln_tr_done[1])])
    WV_OK = dWv.n
    ev("gp", lambda e: e.dma_start(wo16, wo), sc=dCn)
    WO16_OK = dCn.n

    pj = 0
    kh_done = {}
    for kh in range(HK):
        par = kh % 2
        if kh < 2:
            ld = wk_ld[kh]
        else:
            ev("gp", lambda e, kh=kh, par=par: e.dma_start(
                wblk[:, par, :, :],
                wk[:, kh * 128:(kh + 1) * 128].rearrange("(c p) n -> p c n",
                                                         p=128)),
               sc=dWb[par], waits=[(c_pe, kh_done[kh - 2])])
            ld = dWb[par].n
        for tt in range(4):
            bk = 2 + pj % 4
            for kc in range(NCD):
                w0 = []
                if kc == 0:
                    w0 = [(dWb[par], ld), (c_ev, h1_ev_q[tt])] + bw(bk)
                ev("pe", lambda e, par=par, tt=tt, kc=kc, bk=bk: e.matmul(
                    bankf(bk), wblk[:, par, kc, :],
                    h1[:, kc, tt * 512:(tt + 1) * 512],
                    start=(kc == 0), stop=(kc == NCD - 1)),
                   inc=(c_pe if kc == NCD - 1 else None), waits=w0)
            ev("ve", lambda e, kh=kh, tt=tt, bk=bk: e.tensor_copy(
                k_fm[:, kh, 512 + tt * 512:512 + (tt + 1) * 512], bankf(bk)),
               inc=c_ve, waits=[(c_pe, c_pe.n), (c_gp, PADS)])
            brel(bk, c_ve)
            pj += 1
        kh_done[kh] = c_pe.n
    for tc in range(NTC):
        bk = 2 + pj % 4
        for kc in range(NCD):
            w0 = []
            if kc == 0:
                w0 = [(dWv, WV_OK), (c_ev, h1_ev_q[tc // 4])] + bw(bk)
            ev("pe", lambda e, tc=tc, kc=kc, bk=bk: e.matmul(
                bankf(bk), h1[:, kc, tc * 128:(tc + 1) * 128], wv_rs[:, kc, :],
                start=(kc == 0), stop=(kc == NCD - 1)),
               inc=(c_pe if kc == NCD - 1 else None), waits=w0)
        ev("ve", lambda e, tc=tc, bk=bk: e.tensor_copy(
            v_tm[:, 4 + tc, :], bankf(bk)),
           inc=c_ve, waits=[(c_pe, c_pe.n), (c_gp, PADS)])
        brel(bk, c_ve)
        pj += 1
    VK_EV = c_ve.n
    h_done = {}
    qev_done = {}
    for h in range(H):
        par = h % 2
        wfree = [(c_pe, h_done[h - 2])] if h >= 2 else [
            (c_pe, kh_done[2 + par])]
        ev("gp", lambda e, h=h, par=par: e.dma_start(
            wblk[:, par, :, :],
            wq[:, h * 128:(h + 1) * 128].rearrange("(c p) n -> p c n", p=128)),
           sc=dWb[par], waits=wfree)
        ld = dWb[par].n
        for tt in range(4):
            bk = 2 + pj % 4
            for kc in range(NCD):
                w0 = []
                if kc == 0:
                    w0 = [(dWb[par], ld)] + bw(bk)
                ev("pe", lambda e, par=par, tt=tt, kc=kc, bk=bk: e.matmul(
                    bankf(bk), wblk[:, par, kc, :],
                    h1[:, kc, tt * 512:(tt + 1) * 512],
                    start=(kc == 0), stop=(kc == NCD - 1)),
                   inc=(c_pe if kc == NCD - 1 else None), waits=w0)
            ev("se", lambda e, h=h, tt=tt, bk=bk: e.activation(
                q_fm[:, h, tt * 512:(tt + 1) * 512], bankf(bk), AF.Identity,
                bias=0.0, scale=QSCALE),
               inc=c_ev, waits=[(c_pe, c_pe.n)])
            brel(bk, c_ev)
            pj += 1
        h_done[h] = c_pe.n
        qev_done[h] = c_ev.n
    Q_EV = c_ev.n



    VR0 = c_vr.n
    VD0 = c_vd.n
    NIT = NQ * H

    def it_qc_h(it):
        return it // H, it % H

    VP0 = c_vpte.n

    def emit_qk(it):
        qc, h = it_qc_h(it)
        vm = min(qc, 4)
        p = it % 2
        qsl = q_fm[:, h, qc * 128:(qc + 1) * 128]
        ksl = k_fm[:, h // 4, qc * 128:qc * 128 + WINC]
        scA = PSf[:, p * 1024:p * 1024 + 512]
        scB = PSf[:, p * 1024 + 512:p * 1024 + WINC]
        w0 = [(c_exp, it - 1), (c_max, it - 1)]
        if it == 0:
            w0 += [(c_ve, VK_EV), (c_ev, Q_EV), (dCn, MASKS_OK),
                   (c_gp, PADS)]
        ev("pe", lambda e, qsl=qsl, ksl=ksl, scA=scA: e.matmul(
            scA, qsl, ksl[:, 0:512], start=True, stop=False), waits=w0)
        ev("pe", lambda e, qsl=qsl, ksl=ksl, scB=scB: e.matmul(
            scB, qsl, ksl[:, 512:WINC], start=True, stop=False))
        ev("pe", lambda e, vm=vm, scA=scA: e.matmul(
            scA, id_bf[:, :], s_masks[:, vm, 0:512], start=False, stop=True))
        ev("pe", lambda e, vm=vm, scB=scB: e.matmul(
            scB, id_bf[:, :], s_masks[:, vm, 512:WINC], start=False,
            stop=True), inc=c_qk)

    def emit_pt(it):
        p = it % 2
        p3 = it % 3
        ptb = PSf[:, 2048:2048 + WINC]
        for j in range(5):
            w1 = []
            if j == 0:
                w1 = [(c_exp, it + 1), (c_vd, VD0 + it + 1), (c_pte, it)]
            ev("pe", lambda e, j=j, p3=p3, p=p, ptb=ptb: e.matmul(
                ptb[:, j * 128:(j + 1) * 128],
                s_p[:, p3, j * 128:(j + 1) * 128], s_d[:, p, :],
                start=True, stop=True),
               inc=(c_pt if j == 4 else None), waits=w1)

    def emit_av(it):
        qc, h = it_qc_h(it)
        p = it % 2
        avb = PSf[:, 3072 + p * 512:3072 + p * 512 + 128]
        for j in range(5):
            w1 = []
            if j == 0:
                w1 = [(c_pte, it + 1), (c_ave, it - 1)]
            ev("pe", lambda e, j=j, p=p, qc=qc, kh=h // 4, avb=avb: e.matmul(
                avb, v_tm[:, qc + j, kh * 128:(kh + 1) * 128],
                s_pt[:, p, j * 128:(j + 1) * 128],
                start=(j == 0), stop=(j == 4)),
               inc=(c_av if j == 4 else None), waits=w1)

    def emit_max(it):
        p = it % 2
        scAll = PSf[:, p * 1024:p * 1024 + WINC]
        ev("ve", lambda e, scAll=scAll, p=p: e.reduce_max(
            negmax[:, p:p + 1], scAll, AX.X, negate=True),
           inc=c_max,
           waits=[(c_qk, it + 1)] + ([(dCn, ID_OK)] if it == 0 else []))

    def emit_rd(it):
        p = it % 2
        ev("ve", lambda e, p=p: e.reciprocal(rsum[:, p:p + 1],
                                             esum[:, p:p + 1]),
           inc=c_vr, waits=[(c_exp, it + 1)])
        ev("ve", lambda e, p=p: e.tensor_scalar(
            s_d[:, p, :], id_bf[:, :], rsum[:, p:p + 1], None, ALU.mult),
           inc=c_vd, waits=[(c_vr, VR0 + it + 1), (c_pt, it - 1)])

    def emit_ave(it):
        qc, h = it_qc_h(it)
        p = it % 2
        avb = PSf[:, 3072 + p * 512:3072 + p * 512 + 128]
        ev("ve", lambda e, p=p, h=h, qc=qc, avb=avb: e.tensor_copy(
            attn[:, h, qc * 128:(qc + 1) * 128], avb),
           inc=c_ave, waits=[(c_av, it + 1)])

    def emit_exp(it):
        p = it % 2
        p3 = it % 3
        scAll = PSf[:, p * 1024:p * 1024 + WINC]
        ev("se", lambda e, scAll=scAll, p3=p3, p=p: e.activation(
            s_p[:, p3, :], scAll, AF.Exp, bias=negmax[:, p:p + 1],
            scale=1.0, accum_out=esum[:, p:p + 1]),
           inc=c_exp, waits=[(c_max, it + 1), (c_pt, it - 2)])

    def emit_pte(it):
        p = it % 2
        ptb = PSf[:, 2048:2048 + WINC]
        ev("se", lambda e, p=p, ptb=ptb: e.copy(s_pt[:, p, :], ptb),
           inc=c_pte, waits=[(c_pt, it + 1), (c_av, it - 1)])

    def emit_pte_ve(it):
        pass

    for m in range(NIT + 4):
        if m < NIT:
            emit_qk(m)
        if 2 <= m <= NIT + 1:
            emit_pt(m - 2)
        if 4 <= m <= NIT + 3:
            emit_av(m - 4)
        if 1 <= m <= NIT:
            emit_rd(m - 1)
        if m < NIT:
            emit_max(m)
        if 4 <= m <= NIT + 3:
            emit_ave(m - 4)
        if 2 <= m <= NIT + 1:
            emit_pte_ve(m - 2)
        if 2 <= m <= NIT + 1:
            emit_pte(m - 2)
        if m < NIT:
            emit_exp(m)
    ATT_DONE = c_ave.n

    for i in range(6):
        bank_rel[i] = None

    wo_done = {}
    for dc in range(NCD):
        par = dc % 2
        wfree = [(c_pe, wo_done[dc - 2])] if dc >= 2 else [
            (c_pe, h_done[H - 2 + par]), (dCn, WO16_OK)]
        ev("sy", lambda e, dc=dc, par=par: e.dma_start(
            wblk[:, par, :, :],
            wo16[:, dc * 128:(dc + 1) * 128].rearrange(
                "(c p) n -> p c n", p=128)),
           sc=dWo[par], waits=wfree)
        ld = dWo[par].n
        for tt in range(4):
            bk = (dc * 4 + tt) % 4
            for af in range(NCD):
                w0 = []
                if af == 0:
                    w0 = [(dWo[par], ld)] + bw(bk)
                    if dc == 0 and tt == 0:
                        w0 += [(c_ave, ATT_DONE), (c_exp, 256), (c_max, 256)]
                ev("pe", lambda e, par=par, tt=tt, af=af, bk=bk: e.matmul(
                    bankf(bk), wblk[:, par, af, :],
                    attn[:, af, tt * 512:(tt + 1) * 512],
                    start=(af == 0), stop=(af == NCD - 1)),
                   inc=(c_pe if af == NCD - 1 else None), waits=w0)
            ev("se", lambda e, dc=dc, tt=tt, bk=bk: e.activation(
                contrib[:, dc, tt * 512:(tt + 1) * 512], bankf(bk), AF.Identity,
                bias=0.0, scale=modall[:, 32 + dc:33 + dc]),
               inc=c_wo, waits=[(c_pe, c_pe.n), (c_mod, 1)])
            brel(bk, c_wo)
        wo_done[dc] = c_pe.n
    WO_DONE = c_wo.n

    pre_steps = []
    for m in range(NM):
        pre_steps.append((gw[:, m * 128:(m + 1) * 128].rearrange(
            "(c p) n -> p c n", p=128), g_scr[m]))
        pre_steps.append((uw[:, m * 128:(m + 1) * 128].rearrange(
            "(c p) n -> p c n", p=128), u_scr[m]))
    for dc in range(NCD):
        for sub in range(3):
            pre_steps.append((
                dw[2048 * sub:2048 * (sub + 1),
                   dc * 128:(dc + 1) * 128].rearrange(
                    "(c p) n -> p c n", p=128),
                d_scr[dc][:, 2048 * sub:2048 * (sub + 1)]))
    st_done = {}
    ld_done = {}
    NPRE = len(pre_steps)
    for si in range(NPRE + 1):
        if si < NPRE:
            psrc = pre_steps[si][0]
            par = si % 2
            lp = si % 2
            w = [st_done[si - 2]] if si - 2 in st_done else []
            ev("gp", lambda e, psrc=psrc, par=par: e.dma_start(
                pre8[:, par, :].rearrange("p (c n) -> p c n", n=128), psrc),
               sc=dPre2[lp], waits=w)
            ld_done[si] = (dPre2[lp], dPre2[lp].n)
        sj = si - 1
        if sj >= 0 and sj < NPRE:
            dst = pre_steps[sj][1]
            par = sj % 2
            lp = sj % 2
            ev("gp", lambda e, dst=dst, par=par: e.dma_start(
                dst.bitcast(U32), pre8[:, par, :].bitcast(U32)),
               sc=dScr2[lp], waits=[ld_done[sj]])
            st_done[sj] = (dScr2[lp], dScr2[lp].n)
    PRE_A, PRE_B = dScr2[0].n, dScr2[1].n
    pre_st = dict(st_done)



    epi_done = 0
    gup_mm_done = {}
    dn_prev = {}
    for Q in range(4):
        h2tr0 = c_tr.n
        for t4 in range(4):
            tc = Q * 4 + t4
            par = t4 % 2
            sc_ = tc % 16
            rel = [(c_sq, NTC + tc - 1), (c_vxh, NTC + tc - 1)]
            if Q == 0 and t4 < 2:
                rel = [(c_wo, WO_DONE)]
            ev("sy", lambda e, tc=tc, par=par: e.dma_start(
                s_u2[:, par, :], xin[tc * 128:(tc + 1) * 128, :]),
               sc=dX[par], waits=rel)
            nx = dX[par].n
            tr0 = c_tr.n
            for dcg in range(4):
                bk = 4 + dcg
                for d4 in range(4):
                    dc = dcg * 4 + d4
                    w0 = []
                    if d4 == 0:
                        w0 = [(c_wo, WO_DONE), (c_ut, c_ut.n),
                              (c_fve, epi_done)] + bw(4 + dcg)
                    ev("pe", lambda e, dc=dc, d4=d4, tc=tc, dcg=dcg: e.matmul(
                        bankf(4 + dcg)[:, d4 * 128:(d4 + 1) * 128],
                        contrib[:, dc, tc * 128:(tc + 1) * 128], id_bf[:, :],
                        start=(d4 == 0), stop=(d4 == 3)),
                       inc=(c_tr if d4 == 3 else None), waits=w0)
            for dcg in range(4):
                ev("ve", lambda e, dcg=dcg, par=par: e.tensor_tensor(
                    s_u2[:, par, dcg * 512:(dcg + 1) * 512],
                    s_u2[:, par, dcg * 512:(dcg + 1) * 512],
                    bankf(4 + dcg), ALU.add),
                   inc=c_ut, waits=[(c_tr, tr0 + dcg + 1), (dX[par], nx),
                                    (c_vxh, NTC + tc - 1)])
                brel(4 + dcg, c_ut)
            ev("ve", lambda e, sc_=sc_, par=par: e.reduce_sum(
                sumx[:, sc_:sc_ + 1], s_u2[:, par, :], AX.X), sc=c_ve,
               waits=[(c_ut, c_ut.n)])
            ev("se", lambda e, sc_=sc_, par=par: e.activation(
                sq2[:, :], s_u2[:, par, :], AF.Square, bias=0.0, scale=1.0,
                accum_out=ssq[:, sc_:sc_ + 1]),
               sc=c_sq, waits=[(c_ut, c_ut.n)])
            ev("ve", lambda e, sc_=sc_: e.tensor_scalar(
                smean[:, sc_:sc_ + 1], sumx[:, sc_:sc_ + 1], INV_D, None,
                ALU.mult), waits=[(c_sq, c_sq.n)], sc=c_ve)
            ev("ve", lambda e, sc_=sc_: e.tensor_scalar(
                svar[:, sc_:sc_ + 1], ssq[:, sc_:sc_ + 1], INV_D, EPS,
                ALU.mult, ALU.add), sc=c_ve)
            ev("ve", lambda e, sc_=sc_: e.tensor_tensor(
                tmpa[:, sc_:sc_ + 1], smean[:, sc_:sc_ + 1],
                smean[:, sc_:sc_ + 1], ALU.mult), sc=c_ve)
            ev("ve", lambda e, sc_=sc_: e.tensor_tensor(
                svar[:, sc_:sc_ + 1], svar[:, sc_:sc_ + 1],
                tmpa[:, sc_:sc_ + 1], ALU.subtract), sc=c_ve, inc=c_vst)
            ev("se", lambda e, sc_=sc_: e.sqrt(
                ssd[:, sc_:sc_ + 1], svar[:, sc_:sc_ + 1]),
               inc=c_ssd, waits=[(c_vst, c_vst.n)])
            ev("ve", lambda e, sc_=sc_: e.reciprocal(
                srstd[:, sc_:sc_ + 1], ssd[:, sc_:sc_ + 1]),
               inc=c_vr, waits=[(c_ssd, c_ssd.n)])
            ev("ve", lambda e, sc_=sc_, par=par, t4=t4: e.tensor_scalar(
                s_xh2[:, t4, :], s_u2[:, par, :], smean[:, sc_:sc_ + 1],
                srstd[:, sc_:sc_ + 1], ALU.subtract, ALU.mult),
               inc=c_vxh, waits=[(c_vr, c_vr.n), (c_tr, h2tr0)])
        XH4 = c_vxh.n
        for dc in range(NCD):
            bk = dc % 2
            for t4 in range(4):
                w0 = []
                if t4 == 0:
                    w0 = [(c_vxh, XH4)] + bw(6 + bk)
                ev("pe", lambda e, t4=t4, dc=dc, bk=bk: e.matmul(
                    bankf(6 + bk)[:, t4 * 128:(t4 + 1) * 128],
                    s_xh2[:, t4, dc * 128:(dc + 1) * 128], id_bf[:, :],
                    start=(t4 == 0), stop=(t4 == 3)),
                   inc=(c_tr if t4 == 3 else None), waits=w0)
            ev("se", lambda e, dc=dc, bk=bk: e.activation(
                h2[:, dc, :], bankf(6 + bk), AF.Identity,
                bias=modall[:, 48 + dc:49 + dc],
                scale=modall[:, 64 + dc:65 + dc]),
               inc=c_h2, waits=[(c_tr, c_tr.n)])
            brel(6 + bk, c_h2)
        H2Q = c_h2.n
        stt_done = {}
        for m in range(NM):
            par = m % 2
            gidx = Q * NM + m
            w0 = [pre_st[2 * m]]
            if Q == 0 and m < 2:
                w0 += [(c_wo, WO_DONE)]
            if gidx >= 2:
                w0 += [(c_fpe, gup_mm_done[gidx - 2])]
            ev("sy", lambda e, m=m, par=par: e.dma_start(g8[:, par, :],
                                                         g_scr[m]),
               sc=dG[par], waits=w0)
            ldg = dG[par].n
            ev("sy", lambda e, m=m, par=par: e.dma_start(u8w[:, par, :],
                                                         u_scr[m]),
               sc=dU[par], waits=[pre_st[2 * m + 1]])
            ldu = dU[par].n
            for kc in range(NCD):
                w1 = []
                if kc == 0:
                    w1 = [(dG[par], ldg), (c_h2, H2Q)] + bw(0 + par)
                ev("pe", lambda e, par=par, kc=kc: e.matmul(
                    bankf(0 + par),
                    g8[:, par, :].rearrange("p (c n) -> p c n", c=16)[
                        :, kc, :].bitcast(F8),
                    h2[:, kc, :], start=(kc == 0), stop=(kc == NCD - 1)),
                   waits=w1)
            for kc in range(NCD):
                w1 = []
                if kc == 0:
                    w1 = [(dU[par], ldu)] + bw(2 + par)
                ev("pe", lambda e, par=par, kc=kc: e.matmul(
                    bankf(2 + par),
                    u8w[:, par, :].rearrange("p (c n) -> p c n", c=16)[
                        :, kc, :].bitcast(F8),
                    h2[:, kc, :], start=(kc == 0), stop=(kc == NCD - 1)),
                   inc=(c_fpe if kc == NCD - 1 else None), waits=w1)
            gup_mm_done[gidx] = c_fpe.n
            w2 = [(c_fpe, c_fpe.n), (c_mod, 1)]
            if m >= 2:
                w2 += [(c_fve, stt_done[m - 2])]
            elif Q > 0:
                w2 += [(c_fve, epi_done)]
            ev("se", lambda e, m=m, par=par: e.activation(
                sy2[:, par, :], bankf(0 + par), AF.Silu, bias=0.0,
                scale=gsus[:, m:m + 1]),
               inc=c_fse, waits=w2)
            brel(0 + par, c_fse)
            ev("ve", lambda e, m=m, par=par: e.scalar_tensor_tensor(
                fuse[:, m, :], bankf(2 + par), gsus[:, 48 + m:49 + m],
                sy2[:, par, :], ALU.mult, ALU.mult),
               inc=c_fve,
               waits=[(c_fse, c_fse.n)]
               + ([(c_ev, 4)] if (Q == 0 and m == 0) else []))
            brel(2 + par, c_fve)
            stt_done[m] = c_fve.n
        FUSEQ = c_fve.n
        GUP_EV = c_fse.n
        dn_mm_done = {}
        yev_done = {}
        stt_y = {}
        trg_after = {}

        def emit_downmm(dc):
            par = dc % 2
            didx = Q * NCD + dc
            w0 = [pre_st[96 + 3 * dc + 1], pre_st[96 + 3 * dc + 2]]
            if Q == 0 and dc < 2:
                w0 += [(c_av, 256), (c_pt, 256), (c_qk, 256)]
            elif dc < 2:
                w0 += [(c_fpe, dn_prev[14 + dc])]
            else:
                w0 += [(c_fpe, dn_mm_done[dc - 2])]
            ev("sy", lambda e, dc=dc, par=par: e.dma_start(
                d8[:, par, :], d_scr[dc]),
               sc=dD8[par], waits=w0)
            ldd = dD8[par].n
            dbk = dc % 2
            for m_ in range(NM):
                w1 = []
                if m_ == 0:
                    w1 = [(dD8[par], ldd), (c_fve, FUSEQ)] + bw(dbk)
                ev("pe", lambda e, par=par, m=m_, dbk=dbk: e.matmul(
                    bankf(dbk),
                    d8[:, par, :].rearrange("p (a n) -> p a n", a=NM)[
                        :, m, :].bitcast(F8),
                    fuse[:, m, :], start=(m == 0), stop=(m == NM - 1)),
                   inc=(c_fpe if m_ == NM - 1 else None), waits=w1)
            dn_mm_done[dc] = c_fpe.n
            w2 = [(c_fpe, c_fpe.n)]
            if Q == 0 and dc == 0:
                w2 += [(c_mod, 1)]
            if dc >= 2:
                w2 += [(c_tr, yev_done[dc - 2])]
            ev("ve", lambda e, par=par, dc=dc, dbk=dbk, Q=Q:
               e.scalar_tensor_tensor(
                   sy2[:, par, :], bankf(dbk), gbds[:, dc:dc + 1],
                   contrib[:, dc, Q * 512:(Q + 1) * 512],
                   ALU.mult, ALU.add),
               inc=c_fve, waits=w2)
            brel(dbk, c_fve)
            stt_y[dc] = c_fve.n

        def emit_transp(dc):
            par = dc % 2
            d4 = dc % 4
            for t4 in range(4):
                w3 = []
                if t4 == 0:
                    w3 = [(c_fve, stt_y[dc])]
                    if d4 == 0:
                        w3 += bw(4) + bw(5) + bw(6) + bw(7)
                ev("pe", lambda e, t4=t4, par=par, d4=d4: e.matmul(
                    bankf(4 + t4)[:, d4 * 128:(d4 + 1) * 128],
                    sy2[:, par, t4 * 128:(t4 + 1) * 128], id_bf[:, :],
                    start=True, stop=True),
                   inc=(c_tr if t4 == 3 else None), waits=w3)
            yev_done[dc] = c_tr.n

        def emit_xe(dcg):
            guard = FUSEQ if dcg < 2 else epi_add_after[dcg - 2]
            for t4 in range(4):
                tc = Q * 4 + t4
                ev("gp", lambda e, tc=tc, t4=t4, dcg=dcg: e.dma_start(
                    s_xe2[:, t4, dcg % 2, :],
                    xin[tc * 128:(tc + 1) * 128, dcg * 512:(dcg + 1) * 512]),
                   sc=dXE, waits=([(c_fve, guard)] if t4 == 0 else []))
            xe_cnt[dcg] = dXE.n

        def emit_epilogue(dcg):
            TRG = trg_after[dcg]
            for t4 in range(4):
                tc = Q * 4 + t4
                g = Q * 4 + dcg
                ev("ve", lambda e, t4=t4, dcg=dcg: e.tensor_tensor(
                    x3p[:, t4, :], s_xe2[:, t4, dcg % 2, :],
                    bankf(4 + t4), ALU.add),
                   inc=c_fve,
                   waits=[(c_tr, TRG), (dXE, xe_cnt[dcg]),
                          (dOL[t4], 16 * g)])
                brel(4 + t4, c_fve)
                ev("gp", lambda e, tc=tc, dcg=dcg, t4=t4: e.dma_start(
                    yout[tc * 128:(tc + 1) * 128,
                         dcg * 512:(dcg + 1) * 512], x3p[:, t4, :]),
                   sc=dOL[t4], waits=[(c_fve, c_fve.n)])
            epi_add_after[dcg] = c_fve.n

        xe_cnt = {}
        epi_add_after = {}
        for dc in range(NCD + 1):
            if dc < NCD:
                if dc % 4 == 0:
                    emit_xe(dc // 4)
                emit_downmm(dc)
            if dc >= 1:
                emit_transp(dc - 1)
                if (dc - 1) % 4 == 3:
                    trg_after[(dc - 1) // 4] = c_tr.n
                    emit_epilogue((dc - 1) // 4)
                dn_prev = dn_mm_done
        epi_done = c_fve.n

    drain = [(s, s.n) for s in dOL] + [(dO, dO.n)]
    ev("sy", lambda e: e.nop(), waits=drain)
    ev("gp", lambda e: e.nop(), waits=drain)

    for name in DBG:
        tap = {
            "h1": (h1, c_ev, H1_EV),
            "q": (q_fm, c_ve, QKV_EV),
            "attn": (attn, c_ave, ATT_DONE),
            "contrib": (contrib, c_wo, WO_DONE),
            "modall": (modall, c_mod, 1),
            "kf": (k_fm, c_ve, QKV_EV),
            "vt": (v_tm, c_ve, QKV_EV),
        }[name]
        tsr, ctr, cnt = tap
        flat = tsr.ap()
        if len(flat.shape) == 3:
            flat = flat.rearrange("p a b -> p (a b)")
        out_t = nc.dram_tensor("dbg_" + name, list(flat.shape), F32,
                               kind="ExternalOutput").ap()
        ev("gp", lambda e, flat=flat, out_t=out_t: e.dma_start(out_t, flat),
           sc=dO, waits=[(ctr, cnt)])
        ev("gp", lambda e, n=dO.n: e.nop(), waits=[(dO, dO.n)])

    _assemble()
    stack.close()
    return nc


def _host_consts():
    i = np.arange(128)[:, None]
    j = np.arange(WINC)[None, :]
    band = (j >= i + 1) & (j <= i + 512)
    masks = np.empty((5, 128, WINC), np.float32)
    for v in range(5):
        cutoff = 512 - v * 128 if v < 4 else 0
        allowed = band & (j >= cutoff)
        masks[v] = np.where(allowed, 0.0, -30000.0).astype(np.float32)
    return masks, np.eye(128, dtype=np.float32)


def _pad_vec(v, rows):
    out = np.zeros((128, 128), np.float32)
    out[:rows] = np.asarray(v, np.float32).reshape(rows, 128)
    return out


def kernel(**inputs):
    f32 = lambda a: np.ascontiguousarray(np.asarray(a, np.float32))
    x = f32(inputs["L0_attn_x"])
    masks, ident = _host_consts()
    common = {
        "wq": f32(inputs["L0_attn_Wq"]),
        "wk": f32(inputs["L0_attn_Wk"]),
        "wv": f32(inputs["L0_attn_Wv"]),
        "wo": f32(inputs["L0_attn_Wo"]),
        "wmod": f32(inputs["L0_wmod"]),
        "gw": f32(inputs["L0_ffn_gate_w"]),
        "uw": f32(inputs["L0_ffn_up_w"]),
        "dw": f32(inputs["L0_ffn_down_w"]),
        "gs_p": _pad_vec(inputs["L0_ffn_gate_scale"], 48),
        "us_p": _pad_vec(inputs["L0_ffn_up_scale"], 48),
        "ds_p": _pad_vec(inputs["L0_ffn_down_scale"], 16),
        "ident": ident,
        "masks": masks,
    }
    c_all = f32(inputs["L0_c"])
    if "nc" not in _CACHE:
        _CACHE["nc"] = build()
    nc = _CACHE["nc"]
    in_maps = []
    for b in range(8):
        m = dict(common)
        m["x"] = np.ascontiguousarray(x[b])
        m["c_p"] = _pad_vec(c_all[b], 16)
        in_maps.append(m)
    res = run_bass_kernel_spmd(nc, in_maps, core_ids=list(range(8)))
    return np.stack([res.results[b]["y"] for b in range(8)], axis=0)
